# version 02
# baseline (speedup 1.0000x reference)
import numpy as np
import ml_dtypes
import concourse.bass as bass
import concourse.mybir as mybir
from concourse.bass_utils import run_bass_kernel_spmd

F32 = mybir.dt.float32
BF16 = mybir.dt.bfloat16
I8 = mybir.dt.int8
I32 = mybir.dt.int32
AF = mybir.ActivationFunctionType
ALU = mybir.AluOpType
AX = mybir.AxisListType
BF = ml_dtypes.bfloat16


def make_cfg(D=4096, H=11008, T=2048, B=4, NHA=4, NHB=32, ladder=768):
    cfg = dict(D=D, H=H, T=T, B=B, NHA=NHA, NHB=NHB, HDB=128, ladder=ladder)
    cfg["HDA"] = D // NHA
    cfg["KD"] = D // 128
    cfg["KH"] = H // 128
    cfg["NQT"] = T // 128
    cfg["TL"] = T // 2
    cfg["NLT"] = cfg["NQT"] // 2
    assert cfg["HDA"] % 128 == 0 and NHB * 128 == D and H % 128 == 0
    owner = [0 if (i % 4) in (0, 3) else 1 for i in range(cfg["NQT"])]
    sets = [[i for i in range(cfg["NQT"]) if owner[i] == p] for p in (0, 1)]
    cfg["owner"], cfg["sets"] = owner, sets
    cfg["lidx"] = {g: li for p in (0, 1) for li, g in enumerate(sets[p])}
    nlt = cfg["NLT"]
    cfg["E256"] = [max(sets[0][2 * m + 1], sets[1][2 * m + 1]) + 1
                   for m in range(nlt // 2)]
    cfg["EB"] = [(max(sets[0][k], sets[1][k]) * 128 + 128 + 511) // 512
                 for k in range(nlt)]
    return cfg


CFG = make_cfg()
PHASES = {0, 1, 2, 3, 4}
DEBUG_XRES = False



def _int8_T(qw_int32):
    t = np.ascontiguousarray(np.asarray(qw_int32).T)
    return np.ascontiguousarray(t.view(np.int8)[:, 0::4])


def _masks_a(cfg, parity):
    L = cfg["sets"][parity]
    n_m = cfg["NLT"] // 2
    out = np.zeros((max(n_m, 1), cfg["NQT"], 128, 256), dtype=BF)
    ss = np.arange(128)
    for m in range(n_m):
        for j in range(cfg["E256"][m]):
            blk = np.zeros((128, 256), dtype=np.float32)
            for sub in range(2):
                tglob = L[2 * m + sub] * 128 + np.arange(128)
                blk[:, sub * 128:(sub + 1) * 128] = (
                    (j * 128 + ss)[:, None] <= tglob[None, :]).astype(np.float32)
            out[m, j] = blk.astype(BF)
    return out


def _masks_b(cfg, parity):
    L = cfg["sets"][parity]
    out = np.zeros((cfg["NLT"], 128, 512), dtype=BF)
    for k in range(cfg["NLT"]):
        j = cfg["EB"][k] - 1
        tglob = L[k] * 128 + np.arange(128)
        sglob = j * 512 + np.arange(512)
        out[k] = np.where(sglob[None, :] <= tglob[:, None], 0.0, -1e9).astype(BF)
    return out



STREAMS = ("pe", "dve", "act", "pool", "qs", "qa", "qp")
STREAM_ENGINE = {"pe": "tensor", "dve": "vector", "act": "scalar",
                 "pool": "gpsimd", "qs": "sync", "qa": "scalar", "qp": "gpsimd"}
ENGINES = ("tensor", "vector", "scalar", "gpsimd", "sync")
DMA_STREAMS = ("qs", "qa", "qp")


class Op:
    __slots__ = ("idx", "stream", "emit", "deps", "inc", "seq", "tag")

    def __init__(self, idx, stream, emit, tag):
        self.idx, self.stream, self.emit, self.tag = idx, stream, emit, tag
        self.deps = []
        self.inc = stream in DMA_STREAMS
        self.seq = None


class Region:
    __slots__ = ("writer", "readers", "overlaps")

    def __init__(self):
        self.writer = None
        self.readers = []
        self.overlaps = []


class Prog:
    def __init__(self):
        self.ops = []
        self.regions = {}
        self.stash = {}

    def region(self, tensor, lo=0, hi=1 << 60):
        key = (tensor, lo, hi)
        r = self.regions.get(key)
        if r is None:
            r = Region()
            for (t2, l2, h2), r2 in self.regions.items():
                if t2 == tensor and not (hi <= l2 or h2 <= lo):
                    r.overlaps.append(r2)
                    r2.overlaps.append(r)
            self.regions[key] = r
        return r

    def add(self, stream, emit, reads=(), writes=(), deps=(), tag=""):
        me = Op(len(self.ops), stream, emit, tag)
        dset = me.deps

        def dep(op):
            if op is not None and op is not me and op not in dset:
                assert op.idx < me.idx
                dset.append(op)

        for d in deps:
            dep(d)
        for r in reads:
            dep(r.writer)
            for r2 in r.overlaps:
                dep(r2.writer)
        for r in writes:
            dep(r.writer)
            for rd in r.readers:
                dep(rd)
            for r2 in r.overlaps:
                dep(r2.writer)
                for rd in r2.readers:
                    dep(rd)
        self.ops.append(me)
        for r in reads:
            r.readers.append(me)
        for r in writes:
            r.writer = me
            r.readers = []
        return me

    NSUB = 4

    def emit_all(self, sems):
        for op in self.ops:
            for d in op.deps:
                if d.stream == "pe" and op.stream == "pe":
                    continue
                d.inc = True
        counts = {s: 0 for s in STREAMS}
        dma_idx = {s: 0 for s in DMA_STREAMS}
        for op in self.ops:
            if op.stream in DMA_STREAMS:
                i = dma_idx[op.stream]
                dma_idx[op.stream] = i + 1
                op.seq = (i % Prog.NSUB, 16 * (i // Prog.NSUB + 1))
            elif op.inc:
                counts[op.stream] += 1
                op.seq = counts[op.stream]
        per_engine = {e: [] for e in ENGINES}
        for op in self.ops:
            per_engine[STREAM_ENGINE[op.stream]].append(op)
        wm = {}

        def _wait(eng, ename, stream, sub, val):
            key = (ename, stream, sub)
            if val > wm.get(key, 0):
                sem = sems[stream][sub] if sub is not None else sems[stream]
                eng.wait_ge(sem, val)
                wm[key] = val

        def section(eng, ename):
            for op in per_engine[ename]:
                for d in op.deps:
                    if d.seq is None:
                        continue
                    if d.stream in DMA_STREAMS:
                        _wait(eng, ename, d.stream, d.seq[0], d.seq[1])
                    elif not (d.stream == "pe" and op.stream == "pe"):
                        _wait(eng, ename, d.stream, None, d.seq)
                if op.stream in DMA_STREAMS:
                    sub, val = op.seq
                    _wait(eng, ename, op.stream, sub, val - 16)
                inst = op.emit(eng)
                if op.stream in DMA_STREAMS:
                    sub, val = op.seq
                    if inst is None:
                        inst = eng.nop()
                    inst.then_inc(sems[op.stream][sub], 16)
                elif op.inc:
                    if inst is None:
                        inst = eng.nop()
                    inst.then_inc(sems[op.stream], 1)
            if ename == "sync":
                for s in DMA_STREAMS:
                    n = dma_idx[s]
                    for sub in range(Prog.NSUB):
                        k = (n - sub + Prog.NSUB - 1) // Prog.NSUB
                        if k > 0:
                            _wait(eng, ename, s, sub, 16 * k)
                for s in STREAMS:
                    if s not in DMA_STREAMS and counts[s] > 0:
                        _wait(eng, ename, s, None, counts[s])
        return section




def build_program(cfg):
    D, H, T, TL = cfg["D"], cfg["H"], cfg["T"], cfg["TL"]
    KD, KH, NLT, NSB = cfg["KD"], cfg["KH"], cfg["NLT"], cfg["NQT"]
    NHA, HDA, NHB = cfg["NHA"], cfg["HDA"], cfg["NHB"]
    KA = HDA // 128
    NM = NLT // 2
    TH = TL // 2
    KT = TH // 128
    E256, EB = cfg["E256"], cfg["EB"]
    owner, lidx = cfg["owner"], cfg["lidx"]

    nc = bass.Bass("TRN2", target_bir_lowering=False,
                   detect_race_conditions=False)

    def din(name, shape, dt):
        return nc.dram_tensor(name, shape, dt, kind="ExternalInput").ap()

    x_my = din("x_my", [TL, D], F32)
    x_full = din("x_full", [T, D], F32)
    qgT = din("qgT", [H, D], I8)
    quT = din("quT", [H, D], I8)
    qdT = din("qdT", [D, H], I8)
    zgT = din("zgT", [H, KD], I32)
    sgT = din("sgT", [H, KD], F32)
    zuT = din("zuT", [H, KD], I32)
    suT = din("suT", [H, KD], F32)
    zdT = din("zdT", [D, KH], I32)
    sdT = din("sdT", [D, KH], F32)
    bvq = din("bvq", [NHA, HDA], F32)
    bvk = din("bvk", [NHA, HDA], F32)
    bvv = din("bvv", [NHA, HDA], F32)
    wq = din("wq", [D, D], F32)
    wk = din("wk", [D, 128], F32)
    wv = din("wv", [D, 128], F32)
    wo = din("wo", [D, D], F32)
    runtag = din("runtag", [2, 4], I32)
    mska = din("mska", [max(NM, 1), NSB, 128, 256], BF16)
    mskb = din("mskb", [NLT, 128, 512], BF16)
    identity = din("identity", [128, 128], BF16)

    out = nc.dram_tensor("out", [TL, D], F32, kind="ExternalOutput").ap()
    xdbg = nc.dram_tensor("xdbg", [TL, D], F32, kind="ExternalOutput").ap() \
        if DEBUG_XRES else None

    wg_d = nc.dram_tensor("wg_d", [H, D], BF16, kind="Internal").ap()
    wu_d = nc.dram_tensor("wu_d", [H, D], BF16, kind="Internal").ap()
    wd_d = nc.dram_tensor("wd_d", [D, H], BF16, kind="Internal").ap()
    ln1_d = nc.dram_tensor("ln1_d", [T, D], BF16, kind="Internal").ap()
    lnq_d = nc.dram_tensor("lnq_d", [TL, D], BF16, kind="Internal").ap()
    bind_d = nc.dram_tensor("bind_d", [2, NHA, HDA], F32, kind="Internal").ap()
    x_res = nc.dram_tensor("x_res", [TL, D], F32, kind="Internal").ap()
    kv_sh = nc.dram_tensor("kv_sh", [4, TL, 128], BF16, kind="Internal",
                           addr_space="Shared").ap()
    fl_sh = nc.dram_tensor("fl_sh", [2, 8], I32, kind="Internal",
                           addr_space="Shared").ap()

    p = Prog()
    ARC = 96256

    import contextlib
    with contextlib.ExitStack() as _ctx:
        def _cm(x):
            return _ctx.enter_context(x)
        AR = _cm(nc.sbuf_tensor("AR", [128, ARC], BF16))
        stats = _cm(nc.sbuf_tensor("stats", [128, 64], F32))
        zsb = _cm(nc.sbuf_tensor("zsb", [128, 2 * max(KH, KD)], F32))
        zsb_i = _cm(nc.sbuf_tensor("zsb_i", [128, max(KH, KD)], I32))
        qkbT = _cm(nc.sbuf_tensor("qkbT", [128, KD], F32))
        vbB = _cm(nc.sbuf_tensor("vbB", [128, HDA], BF16))
        ident = _cm(nc.sbuf_tensor("ident", [128, 128], BF16))
        identf = _cm(nc.sbuf_tensor("identf", [128, 128], F32))
        ones1 = _cm(nc.sbuf_tensor("ones1", [1, 128], BF16))
        bvb = _cm(nc.sbuf_tensor("bvb", [NHA, 2 * HDA], F32))
        bvrow = _cm(nc.sbuf_tensor("bvrow", [1, HDA], F32))
        bvrow2 = _cm(nc.sbuf_tensor("bvrow2", [1, HDA], BF16))
        flsb = _cm(nc.sbuf_tensor("flsb", [1, 8], I32))
        flone = _cm(nc.sbuf_tensor("flone", [2, 4], I32))
        flzero = _cm(nc.sbuf_tensor("flzero", [2, 4], I32))
        csts = _cm(nc.sbuf_tensor("csts", [128, 8], F32))
        PSA = _cm(nc.psum_tensor("PSA", [128, 2048], F32))
        PSB = _cm(nc.psum_tensor("PSB", [128, 2048], F32))
        s_pe = _cm(nc.semaphore("s_pe"))
        s_dve = _cm(nc.semaphore("s_dve"))
        s_act = _cm(nc.semaphore("s_act"))
        s_pool = _cm(nc.semaphore("s_pool"))
        s_qs = [_cm(nc.semaphore(f"s_qs{i}")) for i in range(Prog.NSUB)]
        s_qa = [_cm(nc.semaphore(f"s_qa{i}")) for i in range(Prog.NSUB)]
        s_qp = [_cm(nc.semaphore(f"s_qp{i}")) for i in range(Prog.NSUB)]
        s_lad = _cm(nc.semaphore("s_lad"))
        block = _cm(nc.Block())
        sems = {"pe": s_pe, "dve": s_dve, "act": s_act, "pool": s_pool,
                "qs": s_qs, "qa": s_qa, "qp": s_qp}

        def R(lo, cols):
            assert 0 <= lo and lo + cols <= ARC, (lo, cols)
            return p.region("AR", lo, lo + cols)

        def A(lo, cols):
            return AR[:, lo:lo + cols]

        def A8(lo, cols_i8):
            return AR[:, lo:lo + (cols_i8 + 1) // 2].bitcast(I8)[:, 0:cols_i8]

        def A32(lo, cols_f32):
            return AR[:, lo:lo + 2 * cols_f32].bitcast(F32)

        def PR(psn, bank, banks=1):
            return p.region(psn, bank * 512, (bank + banks) * 512)

        def dma(stream, out_ap, in_ap, reads=(), writes=(), deps=(), tag=""):
            return p.add(stream,
                         lambda e, o=out_ap, i=in_ap: e.dma_start(out=o, in_=i),
                         reads=reads, writes=writes, deps=deps, tag=tag)

        def dmaT(stream, out_ap, in_ap, reads=(), writes=(), deps=(), tag=""):
            return p.add(stream,
                         lambda e, o=out_ap, i=in_ap: e.dma_start_transpose(o, i),
                         reads=reads, writes=writes, deps=deps, tag=tag)

        r_ident = p.region("ident")
        r_identf = p.region("identf")
        r_ones = p.region("ones1")
        r_flone = p.region("flone")
        r_flzero = p.region("flzero")
        ld_id = dma("qs", ident[:, :], identity, [], [r_ident], tag="ident")
        p.add("dve", lambda e: e.tensor_copy(identf[:, :], ident[:, :]),
              reads=[r_ident], writes=[r_identf], deps=[ld_id])
        p.add("dve", lambda e: e.memset(ones1[:, :], 1.0), writes=[r_ones])
        dma("qp", flone[:, :], runtag, [], [r_flone], tag="tagld")
        p.add("pool", lambda e: e.memset(flzero[:, :], 0), writes=[r_flzero])
        r_csts = p.region("csts")
        p.add("dve", lambda e: e.memset(csts[:, 0:1], 1e-5), writes=[r_csts])

        r_flW = p.region("fl_shW")

        def _fl_zero(e):
            pid = e.partition_id()
            par = e.snap(pid % 2, min_val=0, max_val=1)
            return e.dma_start(out=fl_sh[:, bass.ds(par * 4, 4)],
                               in_=flzero[0:2, 0:4])
        p.add("qp", _fl_zero, reads=[r_flzero], writes=[r_flW], tag="flz")

        PIECE = 4096
        QI0, QI1 = 91136, 93184
        WQ0, WQ1 = 82944, 87040
        r_qi = [R(QI0, 2048), R(QI1, 2048)]
        qi_ap = [A8(QI0, PIECE), A8(QI1, PIECE)]
        r_wdq = [R(WQ0, 4096), R(WQ1, 4096)]
        wdq_ap = [A(WQ0, 4096), A(WQ1, 4096)]
        r_zs = p.region("zsb")
        r_zsi = p.region("zsb_i")
        wdeq_regions = {}

        def emit_dequant(tag, qsrc, zsrc, ssrc, wdst, n_out, n_in):
            ngr = n_in // 128
            npc = (n_in + PIECE - 1) // PIECE
            for ot in range(n_out):
                ld_z = dma("qp", zsb_i[:, 0:ngr], zsrc[ot * 128:(ot + 1) * 128, :],
                           [], [r_zsi], tag=f"z{tag}{ot}")
                cvt = p.add("pool",
                            lambda e, g=ngr: e.tensor_copy(zsb[:, 0:g],
                                                           zsb_i[:, 0:g]),
                            reads=[r_zsi], writes=[r_zs], deps=[ld_z])
                ld_s = dma("qp", zsb[:, ngr:2 * ngr],
                           ssrc[ot * 128:(ot + 1) * 128, :],
                           [], [r_zs], deps=[cvt], tag=f"s{tag}{ot}")
                for pc in range(npc):
                    w = min(PIECE, n_in - pc * PIECE)
                    sl = (ot * npc + pc) % 2
                    ld_q = dma("qp", qi_ap[sl][:, 0:w],
                               qsrc[ot * 128:(ot + 1) * 128,
                                    pc * PIECE:pc * PIECE + w],
                               [], [r_qi[sl]], tag=f"q{tag}{ot}.{pc}")
                    last = None
                    for c in range(w // 128):
                        g = pc * (PIECE // 128) + c
                        last = p.add(
                            "pool",
                            lambda e, sl=sl, c=c, g=g: e.tensor_scalar(
                                out=wdq_ap[sl][:, c * 128:(c + 1) * 128],
                                in0=qi_ap[sl][:, c * 128:(c + 1) * 128],
                                scalar1=zsb[:, g:g + 1],
                                scalar2=zsb[:, ngr + g:ngr + g + 1],
                                op0=ALU.subtract, op1=ALU.mult),
                            reads=[r_qi[sl], r_zs], writes=[r_wdq[sl]],
                            deps=[ld_q, ld_s], tag=f"dq{tag}{ot}.{pc}.{c}")
                    rr = p.region(f"w{tag}", ot * npc + pc, ot * npc + pc + 1)
                    dma("qa" if (ot + pc) % 2 else "qs",
                        wdst[ot * 128:(ot + 1) * 128, pc * PIECE:pc * PIECE + w],
                        wdq_ap[sl][:, 0:w], [r_wdq[sl]], [rr], deps=[last],
                        tag=f"wst{tag}{ot}.{pc}")
                    wdeq_regions[(tag, ot, pc)] = rr

        if 0 in PHASES:
            emit_dequant("g", qgT, zgT, sgT, wg_d, KH, D)
            emit_dequant("u", quT, zuT, suT, wu_d, KH, D)
            emit_dequant("d", qdT, zdT, sdT, wd_d, KD, H)

        def wsrc_regions(tag, row_lo, row_hi, col_lo, col_hi, n_in):
            npc = (n_in + PIECE - 1) // PIECE
            rs = []
            for ot in range(row_lo // 128, (row_hi + 127) // 128):
                for pc in range(col_lo // PIECE,
                                min(npc, (col_hi + PIECE - 1) // PIECE)):
                    rs.append(wdeq_regions[(tag, ot, pc)])
            return rs

        def emit_ln(x_src_ap, x_reads, xb_lo, junk_lo, dst_lo, sidx, tag):
            xb = A32(xb_lo, D)
            r_xb = R(xb_lo, 2 * D)
            r_junk = R(junk_lo, 2 * (D // 2))
            r_st = p.region("stats", sidx * 8, sidx * 8 + 8)
            st = stats[:, sidx * 8:sidx * 8 + 8]
            r_dst = R(dst_lo, D)
            ld = dma("qs", xb[:, :], x_src_ap, list(x_reads), [r_xb], tag=tag)
            s1 = p.add("dve", lambda e: e.reduce_sum(st[:, 0:1], xb[:, :],
                                                     axis=AX.X),
                       reads=[r_xb], writes=[r_st], deps=[ld])
            p.add("dve", lambda e: e.tensor_scalar_mul(st[:, 1:2], st[:, 0:1],
                                                       -1.0 / D),
                  reads=[r_st], writes=[r_st], deps=[s1])
            sq = []
            for hf in range(2):
                q1 = p.add(
                    "act", lambda e, hf=hf: e.activation(
                        out=A32(junk_lo, D // 2),
                        in_=xb[:, hf * (D // 2):(hf + 1) * (D // 2)],
                        func=AF.Square, bias=st[:, 1:2], scale=1.0),
                    reads=[r_xb, r_st], writes=[r_junk], deps=[s1])
                sq.append(p.add(
                    "dve", lambda e, hf=hf: e.reduce_sum(
                        st[:, 2 + hf:3 + hf], A32(junk_lo, D // 2), axis=AX.X),
                    reads=[r_junk], writes=[r_st], deps=[q1]))
            s3 = p.add("dve", lambda e: e.tensor_add(st[:, 4:5], st[:, 2:3],
                                                     st[:, 3:4]),
                       reads=[r_st], writes=[r_st], deps=sq)
            s4 = p.add("act", lambda e: e.activation(out=st[:, 5:6],
                                                     in_=st[:, 4:5],
                                                     func=AF.Sqrt,
                                                     bias=csts[:, 0:1],
                                                     scale=1.0 / D),
                       reads=[r_st, r_csts], writes=[r_st], deps=[s3])
            s5 = p.add("dve", lambda e: e.reciprocal(st[:, 6:7], st[:, 5:6]),
                       reads=[r_st], writes=[r_st], deps=[s4])
            ln = p.add("dve", lambda e: e.tensor_scalar(
                out=A(dst_lo, D), in0=xb[:, :], scalar1=st[:, 1:2],
                scalar2=st[:, 6:7], op0=ALU.add, op1=ALU.mult),
                reads=[r_xb, r_st], writes=[r_dst], deps=[s5], tag=tag + "f")
            return ln, r_dst

        trsl = PSB[:, 1536:2048].bitcast(BF16)
        r_trsl = PR("PSB", 3)

        def emit_tr(src_lo, dst_lo, deps=(), scalar1=None, evac="dve",
                    slot=0, tag="tr"):
            r_src = R(src_lo, 128)
            r_dst = R(dst_lo, 128)
            mm = p.add("pe", lambda e: e.transpose(
                trsl[:, slot * 128:(slot + 1) * 128], A(src_lo, 128),
                ident[:, :]),
                reads=[r_src, r_ident], writes=[r_trsl], deps=list(deps),
                tag=tag)
            if scalar1 is None:
                if evac == "act":
                    fn = lambda e: e.copy(A(dst_lo, 128),
                                          trsl[:, slot * 128:(slot + 1) * 128])
                else:
                    fn = lambda e: e.tensor_copy(
                        A(dst_lo, 128), trsl[:, slot * 128:(slot + 1) * 128])
                ev = p.add(evac, fn, reads=[r_trsl], writes=[r_dst], deps=[mm],
                           tag=tag + "e")
            else:
                if evac == "act":
                    fn = lambda e: e.mul(A(dst_lo, 128),
                                         trsl[:, slot * 128:(slot + 1) * 128],
                                         scalar1)
                else:
                    fn = lambda e: e.tensor_scalar(
                        out=A(dst_lo, 128),
                        in0=trsl[:, slot * 128:(slot + 1) * 128],
                        scalar1=scalar1, scalar2=None, op0=ALU.mult)
                ev = p.add(evac, fn, reads=[r_trsl], writes=[r_dst], deps=[mm],
                           tag=tag + "e")
            return ev

        r_bvb = p.region("bvb")
        r_bvrow = p.region("bvrow")
        r_qkbT = p.region("qkbT")
        r_vbB = p.region("vbB")
        r_bind = p.region("bind_d")

        ldb = [dma("qs", bvb[:, i * HDA:(i + 1) * HDA], src, [], [r_bvb],
                   tag=f"bv{i}")
               for i, src in enumerate((bvq, bvk))]

        BTMP = 80640

        def bind_seq(col, sidx):
            src = bvb[:, col:col + HDA]
            r_c = p.region("bvb", col, col + HDA)
            st = stats[:, 40 + sidx * 4:44 + sidx * 4]
            r_st = p.region("stats", 40 + sidx * 4, 44 + sidx * 4)
            r_tmp = R(BTMP, HDA)
            a1 = p.add("dve", lambda e: e.reduce_sum(
                st[0:NHA, 0:1], src, axis=AX.X, apply_absolute_value=True),
                reads=[r_c], writes=[r_st], deps=list(ldb))
            a2 = p.add("dve", lambda e: e.tensor_scalar_mul(
                st[0:NHA, 1:2], st[0:NHA, 0:1], 1.0 / HDA),
                reads=[r_st], writes=[r_st], deps=[a1])
            sg = p.add("act", lambda e: e.activation(out=src, in_=src,
                                                     func=AF.Sign),
                       reads=[r_c], writes=[r_c], deps=[a1])
            t1 = p.add("act", lambda e: e.activation(
                out=A(BTMP, HDA)[0:NHA, :], in_=src, func=AF.Square),
                reads=[r_c], writes=[r_tmp], deps=[sg])
            t2 = p.add("dve", lambda e: e.tensor_sub(
                src, src, A(BTMP, HDA)[0:NHA, :]),
                reads=[r_c, r_tmp], writes=[r_c], deps=[t1])
            t3 = p.add("dve", lambda e: e.tensor_scalar_add(src, src, 1.0),
                       reads=[r_c], writes=[r_c], deps=[t2])
            fin = p.add("dve", lambda e: e.tensor_scalar(
                out=src, in0=src, scalar1=st[0:NHA, 1:2], scalar2=None,
                op0=ALU.mult),
                reads=[r_c, r_st], writes=[r_c], deps=[t3, a2])
            return fin

        bq = bind_seq(0, 0)
        bk = bind_seq(HDA, 1)
        qkb = p.add("dve", lambda e: e.tensor_mul(
            bvb[:, 0:HDA], bvb[:, 0:HDA], bvb[:, HDA:2 * HDA]),
            reads=[r_bvb], writes=[p.region("bvb", 0, HDA)], deps=[bq, bk])
        st_qkb = dma("qs", bind_d[0], bvb[0:NHA, 0:HDA], [r_bvb], [r_bind],
                     deps=[qkb], tag="qkbst")
        ldb.append(dma("qs", bvb[:, HDA:2 * HDA], bvv, [r_bvb],
                       [p.region("bvb", HDA, 2 * HDA)], deps=[qkb],
                       tag="bv2"))
        bv_ = bind_seq(HDA, 2)
        st_vb = dma("qs", bind_d[1], bvb[0:NHA, HDA:2 * HDA], [r_bvb],
                    [r_bind], deps=[bv_], tag="vbst")

        QKF = 80128
        r_qkf = R(QKF, 256)
        z0 = p.add("dve", lambda e: e.memset(A32(QKF, 128), 0.0),
                   writes=[r_qkf])
        ld_qk = dma("qs", A32(QKF, 128)[0:KD, 0:128],
                    bind_d[0].rearrange("h a -> (h a)").rearrange(
                        "(c p) -> c p", p=128),
                    [r_bind], [r_qkf], deps=[st_qkb, z0], tag="qkbld")
        ps_qk = PSB[:, 1024:1536]
        r_psqk = PR("PSB", 2)
        tr_qk = p.add("pe", lambda e: e.transpose(
            ps_qk[:, 0:128], A32(QKF, 128), identf[:, :]),
            reads=[r_qkf, r_identf], writes=[r_psqk], deps=[ld_qk])
        ev_qk = p.add("dve", lambda e: e.tensor_copy(qkbT[:, :],
                                                     ps_qk[:, 0:KD]),
                      reads=[r_psqk], writes=[r_qkbT], deps=[tr_qk])

        XB0, XB1, JUNK, LNS = 51456, 59648, 67840, 76032
        r_ln1 = [p.region("ln1_d", g, g + 1) for g in range(NSB)]
        r_lnq = [p.region("lnq_d", k, k + 1) for k in range(NLT)]
        for i in range(NSB + NLT):
            if i < NSB:
                src = x_full[i * 128:(i + 1) * 128, :]
                x_reads = []
            else:
                k = i - NSB
                src = x_my[k * 128:(k + 1) * 128, :]
                x_reads = []
            ln_op, r_lt = emit_ln(src, x_reads, XB0 if i % 2 == 0 else XB1,
                                  JUNK, LNS, 2 + i % 2, tag=f"lnA{i}")
            if i < NSB:
                dma("qa", ln1_d[i * 128:(i + 1) * 128, :], A(LNS, D),
                    [r_lt], [r_ln1[i]], deps=[ln_op], tag=f"l1s{i}")
            else:
                dma("qa", lnq_d[(i - NSB) * 128:(i - NSB + 1) * 128, :],
                    A(LNS, D), [r_lt], [r_lnq[i - NSB]], deps=[ln_op],
                    tag=f"lqs{i}")

        LHB, KSB, QTH, PTA = 0, 16384, 32768, 40960
        YHT, YBLK, LQB, MSK, SIGT = 45056, 47104, 49152, 50176, 50688

        def xres_reg(q, c0, c1):
            return p.region("x_res", q * D + c0, q * D + c1)

        for h in range(NHA):
            ldv = dma("qs", bvrow[0:1, 0:HDA], bind_d[1][h], [r_bind],
                      [r_bvrow], deps=[st_vb], tag=f"vbh{h}")
            cvv = p.add("dve", lambda e: e.tensor_copy(
                bvrow2[0:1, 0:HDA], bvrow[0:1, 0:HDA]),
                reads=[r_bvrow], writes=[p.region("bvrow2")], deps=[ldv])
            vb_evs = []
            VW = min(512, HDA)
            for hf in range((HDA + VW - 1) // VW):
                mm = p.add("pe", lambda e, hf=hf: e.matmul(
                    PSB[:, 0:VW], ones1[:, :],
                    bvrow2[0:1, hf * VW:(hf + 1) * VW],
                    start=True, stop=True),
                    reads=[p.region("bvrow2"), r_ones], writes=[PR("PSB", 0)],
                    deps=[cvv])
                vb_evs.append(p.add("act", lambda e, hf=hf: e.copy(
                    vbB[:, hf * VW:(hf + 1) * VW], PSB[:, 0:VW]),
                    reads=[PR("PSB", 0)], writes=[r_vbB], deps=[mm]))

            r_lhb = [R(LHB + g * HDA, HDA) for g in range(NSB)]
            lhb_ld = [dma("qs", A(LHB + g * HDA, HDA),
                          ln1_d[g * 128:(g + 1) * 128, h * HDA:(h + 1) * HDA],
                          [r_ln1[g]], [r_lhb[g]], tag=f"lh{h}.{g}")
                      for g in range(NSB)]
            k_ev = {}
            for g in range(NSB):
                for dc in range(KA):
                    k_ev[(g, dc)] = emit_tr(
                        LHB + g * HDA + dc * 128, KSB + dc * (NSB * 128) + g * 128,
                        deps=[lhb_ld[g]], slot=(g * KA + dc) % 4,
                        evac="act" if (g + dc) % 2 else "dve",
                        tag=f"K{h}.{g}.{dc}")
            v_op = {}
            for g in range(NSB):
                deps = [k_ev[(g, dc)] for dc in range(KA)] + vb_evs
                v_op[g] = p.add("dve", lambda e, g=g: e.tensor_mul(
                    A(LHB + g * HDA, HDA), A(LHB + g * HDA, HDA), vbB[:, 0:HDA]),
                    reads=[r_lhb[g], r_vbB], writes=[r_lhb[g]], deps=deps,
                    tag=f"V{h}.{g}")

            q_ev = {}
            for k in range(NLT):
                ldq2 = dma("qa", A(LQB, HDA),
                           lnq_d[k * 128:(k + 1) * 128, h * HDA:(h + 1) * HDA],
                           [r_lnq[k]], [R(LQB, HDA)], tag=f"lq{h}.{k}")
                for dc in range(KA):
                    gc = h * KA + dc
                    q_ev[(k, dc)] = emit_tr(
                        LQB + dc * 128, QTH + dc * TL + k * 128,
                        deps=[ldq2, ev_qk], scalar1=qkbT[:, gc:gc + 1],
                        slot=(k + dc) % 4, tag=f"Q{h}.{k}.{dc}")

            for m in range(NM):
                E = E256[m]
                pt_op = {}
                for j in range(E):
                    bank = j % 2
                    ps = PSA[:, bank * 512:bank * 512 + 256]
                    r_ps = PR("PSA", bank)
                    mm = None
                    for dc in range(KA):
                        mm = p.add("pe", lambda e, ps=ps, j=j, dc=dc, m=m:
                                   e.matmul(ps,
                                            A(KSB + dc * NSB * 128 + j * 128, 128),
                                            A(QTH + dc * TL + m * 256, 256),
                                            start=(dc == 0), stop=(dc == KA - 1)),
                                   reads=[R(KSB + dc * NSB * 128 + j * 128, 128),
                                          R(QTH + dc * TL + m * 256, 256)],
                                   writes=[r_ps],
                                   deps=[k_ev[(j, dc)], q_ev[(2 * m, dc)],
                                         q_ev[(2 * m + 1, dc)]],
                                   tag=f"sa{h}.{m}.{j}")
                    sg2 = p.add("act", lambda e, ps=ps: e.activation(
                        out=A(SIGT, 256), in_=ps, func=AF.Sigmoid, scale=0.125),
                        reads=[r_ps], writes=[R(SIGT, 256)], deps=[mm],
                        tag=f"sg{h}.{m}.{j}")
                    msl = MSK + (j % 2) * 256
                    mld = dma("qa", A(msl, 256), mska[m, j], [],
                              [R(msl, 256)], tag=f"ma{m}.{j}")
                    pt_op[j] = p.add("dve", lambda e, j=j, msl=msl: e.tensor_mul(
                        A(PTA + j * 256, 256), A(SIGT, 256), A(msl, 256)),
                        reads=[R(SIGT, 256), R(msl, 256)],
                        writes=[R(PTA + j * 256, 256)], deps=[sg2, mld],
                        tag=f"pt{h}.{m}.{j}")
                ytr = {}
                for dc in range(KA):
                    ps = PSB[:, 512:768]
                    r_ps = PR("PSB", 1)
                    mm = None
                    for j in range(E):
                        mm = p.add("pe", lambda e, ps=ps, j=j, dc=dc, E=E: e.matmul(
                            ps, A(LHB + j * HDA + dc * 128, 128),
                            A(PTA + j * 256, 256),
                            start=(j == 0), stop=(j == E - 1)),
                            reads=[r_lhb[j], R(PTA + j * 256, 256)],
                            writes=[r_ps], deps=[v_op[j], pt_op[j]],
                            tag=f"oa{h}.{m}.{dc}")
                    ev = p.add("act", lambda e, ps=ps, dc=dc: e.copy(
                        A(YHT + dc * 256, 256), ps),
                        reads=[r_ps], writes=[R(YHT + dc * 256, 256)],
                        deps=[mm])
                    for ts in range(2):
                        ytr[(dc, ts)] = emit_tr(
                            YHT + dc * 256 + ts * 128,
                            YBLK + ts * 1024 + dc * 128,
                            deps=[ev], slot=(dc + ts) % 4,
                            tag=f"yb{h}.{m}.{dc}.{ts}")
                for ts in range(2):
                    q = 2 * m + ts
                    deps = [ytr[(dc, ts)] for dc in range(KA)]
                    xb2 = A32(XB0, 2048)
                    r_xb2 = R(XB0, 4096)
                    ld = dma("qs", xb2[:, 0:HDA],
                             x_my[q * 128:(q + 1) * 128, h * HDA:(h + 1) * HDA],
                             [], [r_xb2], deps=deps, tag=f"xa{h}.{q}")
                    ad = p.add("dve", lambda e, ts=ts: e.tensor_add(
                        xb2[:, 0:HDA], xb2[:, 0:HDA], A(YBLK + ts * 1024, HDA)),
                        reads=[r_xb2, R(YBLK + ts * 1024, HDA)],
                        writes=[r_xb2], deps=[ld] + deps)
                    dma("qs", x_res[q * 128:(q + 1) * 128,
                                    h * HDA:(h + 1) * HDA],
                        xb2[:, 0:HDA], [r_xb2],
                        [xres_reg(q, h * HDA, (h + 1) * HDA)], deps=[ad],
                        tag=f"xs{h}.{q}")

        def emit_ffn(x_src, x_dst, dst_name, phase):
            LNT, ACTT = 0, 16384
            SG, ATOK = 60416, 62464
            WM0, WM1 = 64512, 65536
            XS = 66560
            LXB, LJUNK = 70656, 78848
            LLN = 66560
            nhb = (H + 511) // 512
            for th in range(2):
                for kt in range(KT):
                    q = th * KT + kt
                    ln_op, r_lt = emit_ln(
                        x_src[q * 128:(q + 1) * 128, :],
                        [p.region("x_res", q * D, (q + 1) * D)],
                        LXB, LJUNK, LLN, 2 + kt % 2, tag=f"lnF{phase}.{q}")
                    for c in range(KD):
                        emit_tr(LLN + c * 128, LNT + c * TH + kt * 128,
                                deps=[ln_op], slot=c % 4,
                                evac="act" if c % 2 else "dve",
                                tag=f"lt{phase}.{q}.{c}")
                for hb in range(nhb):
                    w = min(512, H - hb * 512)
                    mm_last = {}
                    for tag_m, wsrc, wm_lo, psn, ps_ap in (
                            ("g", wg_d, WM0, "PSA", PSA),
                            ("u", wu_d, WM1, "PSB", PSB)):
                        for c in range(KD):
                            wml = wm_lo + (c % 2) * 512
                            r_wm = R(wml, 512)
                            ldw = dmaT("qs", A(wml, w),
                                       wsrc[hb * 512:hb * 512 + w,
                                            c * 128:(c + 1) * 128],
                                       wsrc_regions(tag_m, hb * 512,
                                                    hb * 512 + w, c * 128,
                                                    (c + 1) * 128, D),
                                       [r_wm],
                                       tag=f"W{tag_m}{phase}.{th}.{hb}.{c}")
                            for kt in range(KT):
                                ps = ps_ap[:, kt * 512:kt * 512 + w]
                                mm_last[(tag_m, kt)] = p.add(
                                    "pe",
                                    lambda e, ps=ps, c=c, kt=kt, wml=wml, w=w:
                                    e.matmul(ps,
                                             A(LNT + c * TH + kt * 128, 128),
                                             A(wml, w), start=(c == 0),
                                             stop=(c == KD - 1)),
                                    reads=[R(LNT + c * TH + kt * 128, 128),
                                           r_wm],
                                    writes=[PR(psn, kt)], deps=[ldw],
                                    tag=f"M{tag_m}{phase}.{th}.{hb}.{c}.{kt}")
                    at_ops = {}
                    for kt in range(KT):
                        sg3 = p.add("act", lambda e, kt=kt, w=w: e.activation(
                            out=A(SG + kt * 512, w),
                            in_=PSA[:, kt * 512:kt * 512 + w], func=AF.Silu),
                            reads=[PR("PSA", kt)], writes=[R(SG + kt * 512, w)],
                            deps=[mm_last[("g", kt)]],
                            tag=f"si{phase}.{th}.{hb}.{kt}")
                        at_ops[kt] = p.add("dve", lambda e, kt=kt, w=w: e.tensor_mul(
                            A(ATOK + kt * 512, w), A(SG + kt * 512, w),
                            PSB[:, kt * 512:kt * 512 + w]),
                            reads=[R(SG + kt * 512, w), PR("PSB", kt)],
                            writes=[R(ATOK + kt * 512, w)],
                            deps=[sg3, mm_last[("u", kt)]],
                            tag=f"am{phase}.{th}.{hb}.{kt}")
                    for kt in range(KT):
                        at = at_ops[kt]
                        for hc in range(w // 128):
                            ch = hb * 4 + hc
                            emit_tr(ATOK + kt * 512 + hc * 128,
                                    ACTT + ch * TH + kt * 128,
                                    deps=[at], slot=(kt + hc) % 4,
                                    evac="act" if hc % 2 else "dve",
                                    tag=f"at{phase}.{th}.{hb}.{kt}.{hc}")
                for db in range(D // 512):
                    mm_last = {}
                    for c in range(KH):
                        wml = WM0 + (c % 2) * 512
                        r_wm = R(wml, 512)
                        ldw = dmaT("qs", A(wml, 512),
                                   wd_d[db * 512:(db + 1) * 512,
                                        c * 128:(c + 1) * 128],
                                   wsrc_regions("d", db * 512, (db + 1) * 512,
                                                c * 128, (c + 1) * 128, H),
                                   [r_wm], tag=f"Wd{phase}.{th}.{db}.{c}")
                        for kt in range(KT):
                            ps = PSA[:, kt * 512:(kt + 1) * 512]
                            mm_last[kt] = p.add(
                                "pe", lambda e, ps=ps, c=c, kt=kt, wml=wml:
                                e.matmul(ps, A(ACTT + c * TH + kt * 128, 128),
                                         A(wml, 512), start=(c == 0),
                                         stop=(c == KH - 1)),
                                reads=[R(ACTT + c * TH + kt * 128, 128), r_wm],
                                writes=[PR("PSA", kt)], deps=[ldw],
                                tag=f"Md{phase}.{th}.{db}.{c}.{kt}")
                    for kt in range(KT):
                        q = th * KT + kt
                        xs = A32(XS + kt * 1024, 512)
                        r_xs = R(XS + kt * 1024, 1024)
                        ld = dma("qs", xs[:, 0:512],
                                 x_src[q * 128:(q + 1) * 128,
                                       db * 512:(db + 1) * 512],
                                 [p.region("x_res", q * D + db * 512,
                                           q * D + (db + 1) * 512)],
                                 [r_xs], tag=f"xf{phase}.{th}.{db}.{kt}")
                        ad = p.add("dve", lambda e, kt=kt: e.tensor_add(
                            A32(XS + kt * 1024, 512)[:, 0:512],
                            A32(XS + kt * 1024, 512)[:, 0:512],
                            PSA[:, kt * 512:(kt + 1) * 512]),
                            reads=[r_xs, PR("PSA", kt)], writes=[r_xs],
                            deps=[ld, mm_last[kt]])
                        dma("qs", x_dst[q * 128:(q + 1) * 128,
                                        db * 512:(db + 1) * 512],
                            xs[:, 0:512], [r_xs],
                            [p.region(dst_name, q * D + db * 512,
                                      q * D + (db + 1) * 512)],
                            deps=[ad], tag=f"xo{phase}.{th}.{db}.{kt}")

        if 2 in PHASES:
            emit_ffn(x_res, x_res, "x_res", 2)

        if 3 in PHASES:
            LNT3, QT3 = 0, 32768
            OUTT = 0
            WQF, WQB = 65536, 73728
            KSB3, VSB3, KVT = 77824, 79872, 81920
            PB3, PTB3, MB3, XS3 = 86016, 88064, 90112, 94208
            L3XB, L3JK, L3LN = 65536, 73728, 81920

            for kt in range(NLT):
                ln_op, _ = emit_ln(x_res[kt * 128:(kt + 1) * 128, :],
                                   [p.region("x_res", kt * D, (kt + 1) * D)],
                                   L3XB, L3JK, L3LN, 2 + kt % 2, tag=f"ln3.{kt}")
                for c in range(KD):
                    emit_tr(L3LN + c * 128, LNT3 + c * TL + kt * 128,
                            deps=[ln_op], slot=c % 4,
                            evac="act" if c % 2 else "dve", tag=f"l3t{kt}.{c}")

            def proj(w_in, col0, dst_lo, tagp):
                ldf = dma("qa", A32(WQF, KD * 128).rearrange("p (c n) -> p c n", c=KD),
                          w_in.rearrange("(c p) n -> p c n", p=128)[:, :,
                                                                   col0:col0 + 128],
                          [], [R(WQF, 2 * KD * 128)], tag=f"pw{tagp}")
                cv = p.add("dve", lambda e: e.tensor_copy(A(WQB, KD * 128),
                                                          A32(WQF, KD * 128)),
                           reads=[R(WQF, 2 * KD * 128)], writes=[R(WQB, KD * 128)],
                           deps=[ldf], tag=f"pc{tagp}")
                evs = []
                PW = min(512, TL)
                for thl in range(TL // PW):
                    mm = None
                    for c in range(KD):
                        mm = p.add("pe", lambda e, c=c, thl=thl: e.matmul(
                            PSB[:, 0:PW], A(WQB + c * 128, 128),
                            A(LNT3 + c * TL + thl * PW, PW),
                            start=(c == 0), stop=(c == KD - 1)),
                            reads=[R(WQB + c * 128, 128),
                                   R(LNT3 + c * TL + thl * PW, PW)],
                            writes=[PR("PSB", 0)], deps=[cv], tag=f"pm{tagp}.{thl}")
                    evs.append(p.add("act", lambda e, thl=thl: e.copy(
                        A(dst_lo + thl * PW, PW), PSB[:, 0:PW]),
                        reads=[PR("PSB", 0)], writes=[R(dst_lo + thl * PW, PW)],
                        deps=[mm], tag=f"pe{tagp}.{thl}"))
                return evs

            q_ev3 = {hq: proj(wq, hq * 128, QT3 + hq * TL, f"q{hq}")
                     for hq in range(NHB)}
            k_ev3 = proj(wk, 0, KSB3, "k")
            v_ev3 = proj(wv, 0, VSB3, "v")

            kv_tr = []
            for slot, src_lo in ((0, KSB3), (1, VSB3)):
                for tt in range(NLT):
                    kv_tr.append(emit_tr(src_lo + tt * 128,
                                         KVT + slot * (NLT * 128) + tt * 128,
                                         deps=(k_ev3 if slot == 0 else v_ev3),
                                         slot=tt % 4, tag=f"kv{slot}.{tt}"))

            r_kvW = p.region("kv_shW")
            r_kvt = R(KVT, 2 * NLT * 128)
            ladder_n = cfg["ladder"]

            def _kv_exchange(e):
                pid = e.partition_id()
                par = e.snap(pid % 2, min_val=0, max_val=1)
                kv4 = kv_sh
                for slot in range(2):
                    e.dma_start(
                        out=kv4[e.snap(par * 2 + slot, min_val=0, max_val=3)]
                        .rearrange("(tt pp) d -> pp tt d", pp=128),
                        in_=A(KVT + slot * NLT * 128, NLT * 128)
                        .rearrange("pp (tt d) -> pp tt d", tt=NLT),
                    ).then_inc(s_lad, 16)
                    e.wait_ge(s_lad, 16 * (slot + 1))
                e.dma_start(out=fl_sh[:, bass.ds(par * 4, 4)],
                            in_=flone[0:2, 0:4]).then_inc(s_lad, 16)
                e.wait_ge(s_lad, 48)
                with (
                    e.register("f0") as f0,
                    e.register("f1") as f1,
                    e.register("tg") as tg,
                    e.register("ptgt") as ptgt,
                    e.register("zr") as zr,
                ):
                    e.reg_mov(f0, 0)
                    e.reg_mov(f1, 0)
                    e.reg_load(tg, flone[0:1, 0:1])
                    e.reg_mov(ptgt, 48)
                    e.reg_alu(zr, par, 1, ALU.logical_shift_right)
                    zidx = e.snap(zr, min_val=0, max_val=1)
                    for _ in range(ladder_n):
                        nd = e.compute_val((f0 != tg) | (f1 != tg))
                        e.dma_start(out=flsb[0:1, 0:8], in_=fl_sh[zidx][None, :],
                                    cond=nd).then_inc(s_lad, 16)
                        c16 = e.compute_val(nd * 16)
                        e.reg_alu(ptgt, ptgt, c16, ALU.add)
                        e.wait_ge(s_lad, ptgt)
                        e.reg_load(f0, flsb[0:1, 0:1])
                        e.reg_load(f1, flsb[0:1, 4:5])
                return e.nop()

            lad_done = p.add("pool", _kv_exchange,
                             reads=[r_kvt, r_flone, p.region("flsb")],
                             writes=[r_kvW, r_flW, p.region("flsb")],
                             deps=kv_tr, tag="kvex")

            vvld, kkld = {}, {}
            for g in range(NSB):
                w_, l_ = owner[g], lidx[g]
                vvld[g] = dma("qp", A(VSB3 + g * 128, 128),
                              kv_sh[2 * w_ + 1, l_ * 128:(l_ + 1) * 128, :],
                              [r_kvW], [R(VSB3 + g * 128, 128)], deps=[lad_done],
                              tag=f"vl{g}")
                ktmp = dma("qp", A(KVT + (g % 2) * 128, 128),
                           kv_sh[2 * w_, l_ * 128:(l_ + 1) * 128, :],
                           [r_kvW], [R(KVT + (g % 2) * 128, 128)],
                           deps=[lad_done], tag=f"kl{g}")
                kkld[g] = emit_tr(KVT + (g % 2) * 128, KSB3 + g * 128,
                                  deps=[ktmp], slot=g % 4, tag=f"kt3{g}")

            mb_ld = [dma("qa", A(MB3 + k * 512, 512), mskb[k], [],
                         [R(MB3 + k * 512, 512)], tag=f"mb{k}")
                     for k in range(NLT)]

            ot_ev = {}
            for h in range(NHB):
                for k in range(NLT):
                    E = EB[k]
                    mm = {}
                    for j in range(E):
                        mm[j] = p.add("pe", lambda e, h=h, k=k, j=j: e.matmul(
                            PSA[:, j * 512:(j + 1) * 512],
                            A(QT3 + h * TL + k * 128, 128),
                            A(KSB3 + j * 512, 512), start=True, stop=True),
                            reads=[R(QT3 + h * TL + k * 128, 128)] +
                            [R(KSB3 + g * 128, 128)
                             for g in range(4 * j, min(4 * j + 4, NSB))],
                            writes=[PR("PSA", j)],
                            deps=[q_ev3[h][(k * 128) // min(512, TL)]] +
                            [kkld[g] for g in range(4 * j, min(4 * j + 4, NSB))],
                            tag=f"sb{h}.{k}.{j}")
                    jb = E - 1
                    mka = p.add("dve", lambda e, jb=jb, k=k: e.tensor_add(
                        PSA[:, jb * 512:(jb + 1) * 512],
                        PSA[:, jb * 512:(jb + 1) * 512], A(MB3 + k * 512, 512)),
                        reads=[PR("PSA", jb), R(MB3 + k * 512, 512)],
                        writes=[PR("PSA", jb)], deps=[mm[jb], mb_ld[k]],
                        tag=f"mk{h}.{k}")
                    r_st = p.region("stats", 0, 8)
                    rmax = p.add("dve", lambda e, E=E: e.reduce_max(
                        stats[:, 0:1], PSA[:, 0:E * 512], axis=AX.X, negate=True),
                        reads=[PR("PSA", 0, E)], writes=[r_st],
                        deps=[mka] + list(mm.values()), tag=f"rm{h}.{k}")
                    rsc = p.add("dve", lambda e: e.tensor_scalar_mul(
                        stats[:, 0:1], stats[:, 0:1], float(128 ** -0.5)),
                        reads=[r_st], writes=[r_st], deps=[rmax])
                    exp_ = p.add("act", lambda e, E=E: e.activation(
                        out=A(PB3, E * 512), in_=PSA[:, 0:E * 512], func=AF.Exp,
                        bias=stats[:, 0:1], scale=float(128 ** -0.5),
                        accum_out=stats[:, 1:2]),
                        reads=[PR("PSA", 0, E), r_st], writes=[R(PB3, 2048), r_st],
                        deps=[rsc], tag=f"ex{h}.{k}")
                    rcp = p.add("dve", lambda e: e.reciprocal(stats[:, 2:3],
                                                              stats[:, 1:2]),
                                reads=[r_st], writes=[r_st], deps=[exp_])
                    pn = p.add("dve", lambda e, E=E: e.tensor_scalar(
                        out=A(PB3, E * 512), in0=A(PB3, E * 512),
                        scalar1=stats[:, 2:3], scalar2=None, op0=ALU.mult),
                        reads=[R(PB3, 2048), r_st], writes=[R(PB3, 2048)],
                        deps=[rcp], tag=f"pn{h}.{k}")
                    pt_t = {}
                    for sc in range(E * 4):
                        pt_t[sc] = emit_tr(PB3 + sc * 128, PTB3 + sc * 128,
                                           deps=[pn], slot=sc % 4,
                                           evac="act" if sc % 2 else "dve",
                                           tag=f"pb{h}.{k}.{sc}")
                    mmo = None
                    for sc in range(min(E * 4, NSB)):
                        mmo = p.add("pe", lambda e, sc=sc, E=E: e.matmul(
                            PSB[:, 512:640], A(VSB3 + sc * 128, 128),
                            A(PTB3 + sc * 128, 128), start=(sc == 0),
                            stop=(sc == min(E * 4, NSB) - 1)),
                            reads=[R(VSB3 + sc * 128, 128),
                                   R(PTB3 + sc * 128, 128)],
                            writes=[PR("PSB", 1)], deps=[vvld[sc], pt_t[sc]],
                            tag=f"ob{h}.{k}.{sc}")
                    ot_ev[(h, k)] = p.add("dve", lambda e, h=h, k=k: e.tensor_copy(
                        A(OUTT + h * TL + k * 128, 128), PSB[:, 512:640]),
                        reads=[PR("PSB", 1)],
                        writes=[R(OUTT + h * TL + k * 128, 128)], deps=[mmo],
                        tag=f"oe{h}.{k}")

            for db in range(D // 512):
                mm_last = {}
                for c in range(NHB):
                    sl = c % 2
                    wml = WQB + sl * 512
                    r_wm = R(wml, 512)
                    ldf = dma("qa", A32(WQF + sl * 1024, 512),
                              wo[c * 128:(c + 1) * 128, db * 512:(db + 1) * 512],
                              [], [R(WQF + sl * 1024, 1024)], tag=f"wo{db}.{c}")
                    cvw = p.add("dve", lambda e, wml=wml, sl=sl: e.tensor_copy(
                        A(wml, 512), A32(WQF + sl * 1024, 512)),
                        reads=[R(WQF + sl * 1024, 1024)], writes=[r_wm],
                        deps=[ldf], tag=f"wc{db}.{c}")
                    for q in range(NLT):
                        psn, ps_ap = ("PSA", PSA) if q < 4 else ("PSB", PSB)
                        bb = q % 4
                        ps = ps_ap[:, bb * 512:(bb + 1) * 512]
                        mm_last[q] = p.add(
                            "pe", lambda e, ps=ps, c=c, q=q, wml=wml: e.matmul(
                                ps, A(OUTT + c * TL + q * 128, 128), A(wml, 512),
                                start=(c == 0), stop=(c == NHB - 1)),
                            reads=[R(OUTT + c * TL + q * 128, 128), r_wm],
                            writes=[PR(psn, bb)], deps=[cvw, ot_ev[(c, q)]],
                            tag=f"wm{db}.{c}.{q}")
                for q in range(NLT):
                    psn, ps_ap = ("PSA", PSA) if q < 4 else ("PSB", PSB)
                    bb = q % 4
                    xs = A32(XS3 + (q % 2) * 1024, 512)
                    r_xs = R(XS3 + (q % 2) * 1024, 1024)
                    ld = dma("qs", xs[:, 0:512],
                             x_res[q * 128:(q + 1) * 128, db * 512:(db + 1) * 512],
                             [p.region("x_res", q * D + db * 512,
                                       q * D + (db + 1) * 512)],
                             [r_xs], tag=f"x3{db}.{q}")
                    ad = p.add("dve", lambda e, ps_ap=ps_ap, bb=bb, xs=xs: e.tensor_add(
                        xs[:, 0:512], xs[:, 0:512],
                        ps_ap[:, bb * 512:(bb + 1) * 512]),
                        reads=[r_xs, PR(psn, bb)], writes=[r_xs],
                        deps=[ld, mm_last[q]])
                    dma("qs", x_res[q * 128:(q + 1) * 128,
                                    db * 512:(db + 1) * 512],
                        xs[:, 0:512], [r_xs],
                        [p.region("x_res", q * D + db * 512,
                                  q * D + (db + 1) * 512)],
                        deps=[ad], tag=f"x3o{db}.{q}")

        if DEBUG_XRES:
            DBB = 91136
            for q in range(NLT):
                xsv = p.region("x_res", q * D, (q + 1) * D)
                tmp = R(DBB, 2048)
                for hf2 in range(2):
                    l1 = dma("qs", A32(DBB, 1024)[:, 0:D // 2],
                             x_res[q * 128:(q + 1) * 128,
                                   hf2 * (D // 2):(hf2 + 1) * (D // 2)],
                             [xsv], [tmp], tag=f"xd{q}.{hf2}a")
                    dma("qs", xdbg[q * 128:(q + 1) * 128,
                                   hf2 * (D // 2):(hf2 + 1) * (D // 2)],
                        A32(DBB, 1024)[:, 0:D // 2],
                        [tmp], [p.region("xdbg", q, q + 1)], deps=[l1],
                        tag=f"xd{q}.{hf2}b")

        if 4 in PHASES:
            emit_ffn(x_res, out, "out", 4)
        else:
            CPB = 91136
            for q in range(NLT):
                xsv = p.region("x_res", q * D, (q + 1) * D)
                tmp = R(CPB, 2048)
                l1 = dma("qs", A32(CPB, 1024)[:, 0:D // 2],
                         x_res[q * 128:(q + 1) * 128, 0:D // 2],
                         [xsv], [tmp], tag=f"cpo{q}a")
                dma("qs", out[q * 128:(q + 1) * 128, 0:D // 2],
                    A32(CPB, 1024)[:, 0:D // 2],
                    [tmp], [p.region("out", q * D, (q + 1) * D)], deps=[l1],
                    tag=f"cpo{q}b")
                l2 = dma("qs", A32(CPB, 1024)[:, 0:D // 2],
                         x_res[q * 128:(q + 1) * 128, D // 2:D],
                         [xsv], [tmp], tag=f"cpo{q}c")
                dma("qs", out[q * 128:(q + 1) * 128, D // 2:D],
                    A32(CPB, 1024)[:, 0:D // 2],
                    [tmp], [p.region("out", q * D, (q + 1) * D)], deps=[l2],
                    tag=f"cpo{q}d")

        section = p.emit_all(sems)

        @block.tensor
        def _(te):
            section(te, "tensor")

        @block.vector
        def _(ve):
            section(ve, "vector")

        @block.scalar
        def _(sc):
            section(sc, "scalar")

        @block.gpsimd
        def _(gp):
            section(gp, "gpsimd")

        @block.sync
        def _(sy):
            section(sy, "sync")

    return nc




def stage_inputs(cfg, inputs):
    x = np.ascontiguousarray(np.asarray(inputs["L0_attn_a_x"], dtype=np.float32))
    qgT = _int8_T(inputs["L0_ffn_qw_gate"])
    quT = _int8_T(inputs["L0_ffn_qw_up"])
    qdT = _int8_T(inputs["L0_ffn_qw_down"])
    zgT = np.ascontiguousarray(np.asarray(inputs["L0_ffn_z_gate"]).T.astype(np.int32))
    sgT = np.ascontiguousarray(np.asarray(inputs["L0_ffn_s_gate"]).T.astype(np.float32))
    zuT = np.ascontiguousarray(np.asarray(inputs["L0_ffn_z_up"]).T.astype(np.int32))
    suT = np.ascontiguousarray(np.asarray(inputs["L0_ffn_s_up"]).T.astype(np.float32))
    zdT = np.ascontiguousarray(np.asarray(inputs["L0_ffn_z_down"]).T.astype(np.int32))
    sdT = np.ascontiguousarray(np.asarray(inputs["L0_ffn_s_down"]).T.astype(np.float32))
    com = dict(
        qgT=qgT, quT=quT, qdT=qdT, zgT=zgT, sgT=sgT, zuT=zuT, suT=suT,
        zdT=zdT, sdT=sdT,
        bvq=np.asarray(inputs["L0_attn_a_bv_q"], dtype=np.float32),
        bvk=np.asarray(inputs["L0_attn_a_bv_k"], dtype=np.float32),
        bvv=np.asarray(inputs["L0_attn_a_bv_v"], dtype=np.float32),
        wq=np.asarray(inputs["L0_attn_b_wq"], dtype=np.float32),
        wk=np.asarray(inputs["L0_attn_b_wk"], dtype=np.float32),
        wv=np.asarray(inputs["L0_attn_b_wv"], dtype=np.float32),
        wo=np.asarray(inputs["L0_attn_b_wo"], dtype=np.float32),
        identity=np.eye(128, dtype=BF),
        runtag=np.full((2, 4), np.random.randint(1, 2**30), dtype=np.int32),
    )
    masks = [(_masks_a(cfg, par), _masks_b(cfg, par)) for par in (0, 1)]
    in_maps = []
    for c in range(2 * cfg["B"]):
        b, par = c // 2, c % 2
        rows = np.concatenate([np.arange(g * 128, (g + 1) * 128)
                               for g in cfg["sets"][par]])
        m = dict(com)
        m["x_my"] = np.ascontiguousarray(x[b][rows])
        m["x_full"] = np.ascontiguousarray(x[b])
        m["mska"], m["mskb"] = masks[par]
        in_maps.append(m)
    return in_maps


def gather_output(cfg, results):
    B, T, D = cfg["B"], cfg["T"], cfg["D"]
    out = np.empty((B, T, D), dtype=np.float32)
    for b in range(B):
        for g in range(cfg["NQT"]):
            par = cfg["owner"][g]
            li = cfg["lidx"][g]
            out[b, g * 128:(g + 1) * 128] = \
                results[2 * b + par]["out"][li * 128:(li + 1) * 128]
    return out


_NC_CACHE = {}


def kernel(**inputs):
    cfg = CFG
    key = "main"
    if key not in _NC_CACHE:
        _NC_CACHE[key] = build_program(cfg)
    nc = _NC_CACHE[key]
    in_maps = stage_inputs(cfg, inputs)
    res = run_bass_kernel_spmd(nc, in_maps, core_ids=list(range(2 * cfg["B"])))
    return gather_output(cfg, res.results)


# version 03
# speedup vs baseline: 1.6765x; 1.6765x over previous
import numpy as np
import ml_dtypes
import concourse.bass as bass
import concourse.mybir as mybir
from concourse.bass_utils import run_bass_kernel_spmd

F32 = mybir.dt.float32
BF16 = mybir.dt.bfloat16
I8 = mybir.dt.int8
I32 = mybir.dt.int32
AF = mybir.ActivationFunctionType
ALU = mybir.AluOpType
AX = mybir.AxisListType
BF = ml_dtypes.bfloat16


def make_cfg(D=4096, H=11008, T=2048, B=4, NHA=4, NHB=32, ladder=768):
    cfg = dict(D=D, H=H, T=T, B=B, NHA=NHA, NHB=NHB, HDB=128, ladder=ladder)
    cfg["HDA"] = D // NHA
    cfg["KD"] = D // 128
    cfg["KH"] = H // 128
    cfg["NQT"] = T // 128
    cfg["TL"] = T // 2
    cfg["NLT"] = cfg["NQT"] // 2
    assert cfg["HDA"] % 128 == 0 and NHB * 128 == D and H % 128 == 0
    owner = [0 if (i % 4) in (0, 3) else 1 for i in range(cfg["NQT"])]
    sets = [[i for i in range(cfg["NQT"]) if owner[i] == p] for p in (0, 1)]
    cfg["owner"], cfg["sets"] = owner, sets
    cfg["lidx"] = {g: li for p in (0, 1) for li, g in enumerate(sets[p])}
    nlt = cfg["NLT"]
    cfg["E256"] = [max(sets[0][2 * m + 1], sets[1][2 * m + 1]) + 1
                   for m in range(nlt // 2)]
    cfg["EB"] = [(max(sets[0][k], sets[1][k]) * 128 + 128 + 511) // 512
                 for k in range(nlt)]
    return cfg


CFG = make_cfg()
PHASES = {0, 1, 2, 3, 4}
DEBUG_XRES = False



def _int8_T(qw_int32):
    t = np.ascontiguousarray(np.asarray(qw_int32).T)
    return np.ascontiguousarray(t.view(np.int8)[:, 0::4])


def _masks_a(cfg, parity):
    L = cfg["sets"][parity]
    n_m = cfg["NLT"] // 2
    out = np.zeros((max(n_m, 1), cfg["NQT"], 128, 256), dtype=BF)
    ss = np.arange(128)
    for m in range(n_m):
        for j in range(cfg["E256"][m]):
            blk = np.zeros((128, 256), dtype=np.float32)
            for sub in range(2):
                tglob = L[2 * m + sub] * 128 + np.arange(128)
                blk[:, sub * 128:(sub + 1) * 128] = (
                    (j * 128 + ss)[:, None] <= tglob[None, :]).astype(np.float32)
            out[m, j] = blk.astype(BF)
    return out


def _masks_b(cfg, parity):
    L = cfg["sets"][parity]
    out = np.zeros((cfg["NLT"], 128, 512), dtype=BF)
    for k in range(cfg["NLT"]):
        j = cfg["EB"][k] - 1
        tglob = L[k] * 128 + np.arange(128)
        sglob = j * 512 + np.arange(512)
        out[k] = np.where(sglob[None, :] <= tglob[:, None], 0.0, -1e9).astype(BF)
    return out



STREAMS = ("pe", "dve", "act", "pool", "qs", "qa", "qp")
STREAM_ENGINE = {"pe": "tensor", "dve": "vector", "act": "scalar",
                 "pool": "gpsimd", "qs": "sync", "qa": "scalar", "qp": "gpsimd"}
ENGINES = ("tensor", "vector", "scalar", "gpsimd", "sync")
DMA_STREAMS = ("qs", "qa", "qp")


class Op:
    __slots__ = ("idx", "stream", "emit", "deps", "inc", "seq", "tag")

    def __init__(self, idx, stream, emit, tag):
        self.idx, self.stream, self.emit, self.tag = idx, stream, emit, tag
        self.deps = []
        self.inc = stream in DMA_STREAMS
        self.seq = None


class Region:
    __slots__ = ("writer", "readers", "overlaps")

    def __init__(self):
        self.writer = None
        self.readers = []
        self.overlaps = []


class Prog:
    def __init__(self):
        self.ops = []
        self.regions = {}
        self.stash = {}

    def region(self, tensor, lo=0, hi=1 << 60):
        key = (tensor, lo, hi)
        r = self.regions.get(key)
        if r is None:
            r = Region()
            for (t2, l2, h2), r2 in self.regions.items():
                if t2 == tensor and not (hi <= l2 or h2 <= lo):
                    r.overlaps.append(r2)
                    r2.overlaps.append(r)
            self.regions[key] = r
        return r

    def add(self, stream, emit, reads=(), writes=(), deps=(), tag=""):
        me = Op(len(self.ops), stream, emit, tag)
        dset = me.deps

        def dep(op):
            if op is not None and op is not me and op not in dset:
                assert op.idx < me.idx
                dset.append(op)

        for d in deps:
            dep(d)
        for r in reads:
            dep(r.writer)
            for r2 in r.overlaps:
                dep(r2.writer)
        for r in writes:
            dep(r.writer)
            for rd in r.readers:
                dep(rd)
            for r2 in r.overlaps:
                dep(r2.writer)
                for rd in r2.readers:
                    dep(rd)
        self.ops.append(me)
        for r in reads:
            r.readers.append(me)
        for r in writes:
            r.writer = me
            r.readers = []
        return me

    NSUB = 4

    def emit_all(self, sems):
        for op in self.ops:
            for d in op.deps:
                if d.stream == "pe" and op.stream == "pe":
                    continue
                d.inc = True
        counts = {s: 0 for s in STREAMS}
        dma_idx = {s: 0 for s in DMA_STREAMS}
        for op in self.ops:
            if op.stream in DMA_STREAMS:
                i = dma_idx[op.stream]
                dma_idx[op.stream] = i + 1
                op.seq = (i % Prog.NSUB, 16 * (i // Prog.NSUB + 1))
            elif op.inc:
                counts[op.stream] += 1
                op.seq = counts[op.stream]
        per_engine = {e: [] for e in ENGINES}
        for op in self.ops:
            per_engine[STREAM_ENGINE[op.stream]].append(op)
        wm = {}

        def _wait(eng, ename, stream, sub, val):
            key = (ename, stream, sub)
            if val > wm.get(key, 0):
                sem = sems[stream][sub] if sub is not None else sems[stream]
                eng.wait_ge(sem, val)
                wm[key] = val

        def section(eng, ename):
            for op in per_engine[ename]:
                for d in op.deps:
                    if d.seq is None:
                        continue
                    if d.stream in DMA_STREAMS:
                        _wait(eng, ename, d.stream, d.seq[0], d.seq[1])
                    elif not (d.stream == "pe" and op.stream == "pe"):
                        _wait(eng, ename, d.stream, None, d.seq)
                if op.stream in DMA_STREAMS:
                    sub, val = op.seq
                    _wait(eng, ename, op.stream, sub, val - 16)
                inst = op.emit(eng)
                if op.stream in DMA_STREAMS:
                    sub, val = op.seq
                    if inst is None:
                        inst = eng.nop()
                    inst.then_inc(sems[op.stream][sub], 16)
                elif op.inc:
                    if inst is None:
                        inst = eng.nop()
                    inst.then_inc(sems[op.stream], 1)
            if ename == "sync":
                for s in DMA_STREAMS:
                    n = dma_idx[s]
                    for sub in range(Prog.NSUB):
                        k = (n - sub + Prog.NSUB - 1) // Prog.NSUB
                        if k > 0:
                            _wait(eng, ename, s, sub, 16 * k)
                for s in STREAMS:
                    if s not in DMA_STREAMS and counts[s] > 0:
                        _wait(eng, ename, s, None, counts[s])
        return section




def build_program(cfg):
    D, H, T, TL = cfg["D"], cfg["H"], cfg["T"], cfg["TL"]
    KD, KH, NLT, NSB = cfg["KD"], cfg["KH"], cfg["NLT"], cfg["NQT"]
    NHA, HDA, NHB = cfg["NHA"], cfg["HDA"], cfg["NHB"]
    KA = HDA // 128
    NM = NLT // 2
    TH = TL // 2
    KT = TH // 128
    E256, EB = cfg["E256"], cfg["EB"]
    owner, lidx = cfg["owner"], cfg["lidx"]

    nc = bass.Bass("TRN2", target_bir_lowering=False,
                   detect_race_conditions=False)

    def din(name, shape, dt):
        return nc.dram_tensor(name, shape, dt, kind="ExternalInput").ap()

    x_my = din("x_my", [TL, D], F32)
    x_full = din("x_full", [T, D], F32)
    qgT = din("qgT", [H, D], I8)
    quT = din("quT", [H, D], I8)
    qdT = din("qdT", [D, H], I8)
    zgT = din("zgT", [H, KD], I32)
    sgT = din("sgT", [H, KD], F32)
    zuT = din("zuT", [H, KD], I32)
    suT = din("suT", [H, KD], F32)
    zdT = din("zdT", [D, KH], I32)
    sdT = din("sdT", [D, KH], F32)
    bvq = din("bvq", [NHA, HDA], F32)
    bvk = din("bvk", [NHA, HDA], F32)
    bvv = din("bvv", [NHA, HDA], F32)
    wq = din("wq", [D, D], F32)
    wk = din("wk", [D, 128], F32)
    wv = din("wv", [D, 128], F32)
    wo = din("wo", [D, D], F32)
    runtag = din("runtag", [2, 4], I32)
    mska = din("mska", [max(NM, 1), NSB, 128, 256], BF16)
    mskb = din("mskb", [NLT, 128, 512], BF16)
    identity = din("identity", [128, 128], BF16)

    out = nc.dram_tensor("out", [TL, D], F32, kind="ExternalOutput").ap()
    xdbg = nc.dram_tensor("xdbg", [TL, D], F32, kind="ExternalOutput").ap() \
        if DEBUG_XRES else None

    wg_d = nc.dram_tensor("wg_d", [H, D], BF16, kind="Internal").ap()
    wu_d = nc.dram_tensor("wu_d", [H, D], BF16, kind="Internal").ap()
    wd_d = nc.dram_tensor("wd_d", [D, H], BF16, kind="Internal").ap()
    ln1_d = nc.dram_tensor("ln1_d", [T, D], BF16, kind="Internal").ap()
    lnq_d = nc.dram_tensor("lnq_d", [TL, D], BF16, kind="Internal").ap()
    bind_d = nc.dram_tensor("bind_d", [2, NHA, HDA], F32, kind="Internal").ap()
    x_res = nc.dram_tensor("x_res", [TL, D], F32, kind="Internal").ap()
    kv_sh = nc.dram_tensor("kv_sh", [4, TL, 128], BF16, kind="Internal",
                           addr_space="Shared").ap()
    fl_sh = nc.dram_tensor("fl_sh", [2, 8], I32, kind="Internal",
                           addr_space="Shared").ap()

    p = Prog()
    ARC = 96256

    import contextlib
    with contextlib.ExitStack() as _ctx:
        def _cm(x):
            return _ctx.enter_context(x)
        AR = _cm(nc.sbuf_tensor("AR", [128, ARC], BF16))
        stats = _cm(nc.sbuf_tensor("stats", [128, 64], F32))
        zsb = _cm(nc.sbuf_tensor("zsb", [128, 2 * max(KH, KD)], F32))
        zsb_i = _cm(nc.sbuf_tensor("zsb_i", [128, max(KH, KD)], I32))
        qkbT = _cm(nc.sbuf_tensor("qkbT", [128, KD], F32))
        vbB = _cm(nc.sbuf_tensor("vbB", [128, HDA], BF16))
        ident = _cm(nc.sbuf_tensor("ident", [128, 128], BF16))
        identf = _cm(nc.sbuf_tensor("identf", [128, 128], F32))
        ones1 = _cm(nc.sbuf_tensor("ones1", [1, 128], BF16))
        bvb = _cm(nc.sbuf_tensor("bvb", [NHA, 2 * HDA], F32))
        bvrow = _cm(nc.sbuf_tensor("bvrow", [1, HDA], F32))
        bvrow2 = _cm(nc.sbuf_tensor("bvrow2", [1, HDA], BF16))
        flsb = _cm(nc.sbuf_tensor("flsb", [1, 8], I32))
        flone = _cm(nc.sbuf_tensor("flone", [2, 4], I32))
        flzero = _cm(nc.sbuf_tensor("flzero", [2, 4], I32))
        csts = _cm(nc.sbuf_tensor("csts", [128, 8], F32))
        PSA = _cm(nc.psum_tensor("PSA", [128, 2048], F32))
        PSB = _cm(nc.psum_tensor("PSB", [128, 2048], F32))
        s_pe = _cm(nc.semaphore("s_pe"))
        s_dve = _cm(nc.semaphore("s_dve"))
        s_act = _cm(nc.semaphore("s_act"))
        s_pool = _cm(nc.semaphore("s_pool"))
        s_qs = [_cm(nc.semaphore(f"s_qs{i}")) for i in range(Prog.NSUB)]
        s_qa = [_cm(nc.semaphore(f"s_qa{i}")) for i in range(Prog.NSUB)]
        s_qp = [_cm(nc.semaphore(f"s_qp{i}")) for i in range(Prog.NSUB)]
        s_lad = _cm(nc.semaphore("s_lad"))
        block = _cm(nc.Block())
        sems = {"pe": s_pe, "dve": s_dve, "act": s_act, "pool": s_pool,
                "qs": s_qs, "qa": s_qa, "qp": s_qp}

        def R(lo, cols):
            assert 0 <= lo and lo + cols <= ARC, (lo, cols)
            return p.region("AR", lo, lo + cols)

        def A(lo, cols):
            return AR[:, lo:lo + cols]

        def A8(lo, cols_i8):
            return AR[:, lo:lo + (cols_i8 + 1) // 2].bitcast(I8)[:, 0:cols_i8]

        def A32(lo, cols_f32):
            return AR[:, lo:lo + 2 * cols_f32].bitcast(F32)

        def PR(psn, bank, banks=1):
            return p.region(psn, bank * 512, (bank + banks) * 512)

        def dma(stream, out_ap, in_ap, reads=(), writes=(), deps=(), tag=""):
            return p.add(stream,
                         lambda e, o=out_ap, i=in_ap: e.dma_start(out=o, in_=i),
                         reads=reads, writes=writes, deps=deps, tag=tag)

        def dmaT(stream, out_ap, in_ap, reads=(), writes=(), deps=(), tag=""):
            return p.add(stream,
                         lambda e, o=out_ap, i=in_ap: e.dma_start_transpose(o, i),
                         reads=reads, writes=writes, deps=deps, tag=tag)

        r_ident = p.region("ident")
        r_identf = p.region("identf")
        r_ones = p.region("ones1")
        r_flone = p.region("flone")
        r_flzero = p.region("flzero")
        ld_id = dma("qs", ident[:, :], identity, [], [r_ident], tag="ident")
        p.add("dve", lambda e: e.tensor_copy(identf[:, :], ident[:, :]),
              reads=[r_ident], writes=[r_identf], deps=[ld_id])
        p.add("dve", lambda e: e.memset(ones1[:, :], 1.0), writes=[r_ones])
        dma("qp", flone[:, :], runtag, [], [r_flone], tag="tagld")
        p.add("pool", lambda e: e.memset(flzero[:, :], 0), writes=[r_flzero])
        r_csts = p.region("csts")
        p.add("dve", lambda e: e.memset(csts[:, 0:1], 1e-5), writes=[r_csts])

        r_flW = p.region("fl_shW")

        def _fl_zero(e):
            pid = e.partition_id()
            par = e.snap(pid % 2, min_val=0, max_val=1)
            return e.dma_start(out=fl_sh[:, bass.ds(par * 4, 4)],
                               in_=flzero[0:2, 0:4])
        p.add("qp", _fl_zero, reads=[r_flzero], writes=[r_flW], tag="flz")

        PIECE = 4096
        QI0, QI1 = 92160, 94208
        WQ0, WQ1 = 83968, 88064
        r_qi = [R(QI0, 2048), R(QI1, 2048)]
        qi_ap = [A8(QI0, PIECE), A8(QI1, PIECE)]
        r_wdq = [R(WQ0, 4096), R(WQ1, 4096)]
        wdq_ap = [A(WQ0, 4096), A(WQ1, 4096)]
        r_zs = p.region("zsb")
        r_zsi = p.region("zsb_i")
        wdeq_regions = {}

        def emit_dequant(tag, qsrc, zsrc, ssrc, wdst, n_out, n_in):
            ngr = n_in // 128
            npc = (n_in + PIECE - 1) // PIECE
            for ot in range(n_out):
                ld_z = dma("qp", zsb_i[:, 0:ngr], zsrc[ot * 128:(ot + 1) * 128, :],
                           [], [r_zsi], tag=f"z{tag}{ot}")
                cvt = p.add("dve",
                            lambda e, g=ngr: e.tensor_copy(zsb[:, 0:g],
                                                           zsb_i[:, 0:g]),
                            reads=[r_zsi], writes=[r_zs], deps=[ld_z])
                ld_s = dma("qp", zsb[:, ngr:2 * ngr],
                           ssrc[ot * 128:(ot + 1) * 128, :],
                           [], [r_zs], deps=[cvt], tag=f"s{tag}{ot}")
                for pc in range(npc):
                    w = min(PIECE, n_in - pc * PIECE)
                    sl = (ot * npc + pc) % 2
                    ld_q = dma("qp", qi_ap[sl][:, 0:w],
                               qsrc[ot * 128:(ot + 1) * 128,
                                    pc * PIECE:pc * PIECE + w],
                               [], [r_qi[sl]], tag=f"q{tag}{ot}.{pc}")
                    last = None
                    deq_eng = "dve"
                    for c in range(w // 128):
                        g = pc * (PIECE // 128) + c
                        last = p.add(
                            deq_eng,
                            lambda e, sl=sl, c=c, g=g: e.tensor_scalar(
                                out=wdq_ap[sl][:, c * 128:(c + 1) * 128],
                                in0=qi_ap[sl][:, c * 128:(c + 1) * 128],
                                scalar1=zsb[:, g:g + 1],
                                scalar2=zsb[:, ngr + g:ngr + g + 1],
                                op0=ALU.subtract, op1=ALU.mult),
                            reads=[r_qi[sl], r_zs], writes=[r_wdq[sl]],
                            deps=[ld_q, ld_s], tag=f"dq{tag}{ot}.{pc}.{c}")
                    rr = p.region(f"w{tag}", ot * npc + pc, ot * npc + pc + 1)
                    dma("qa" if (ot + pc) % 2 else "qs",
                        wdst[ot * 128:(ot + 1) * 128, pc * PIECE:pc * PIECE + w],
                        wdq_ap[sl][:, 0:w], [r_wdq[sl]], [rr], deps=[last],
                        tag=f"wst{tag}{ot}.{pc}")
                    wdeq_regions[(tag, ot, pc)] = rr

        if 0 in PHASES:
            emit_dequant("g", qgT, zgT, sgT, wg_d, KH, D)
            emit_dequant("u", quT, zuT, suT, wu_d, KH, D)
            emit_dequant("d", qdT, zdT, sdT, wd_d, KD, H)

        def wsrc_regions(tag, row_lo, row_hi, col_lo, col_hi, n_in):
            npc = (n_in + PIECE - 1) // PIECE
            rs = []
            for ot in range(row_lo // 128, (row_hi + 127) // 128):
                for pc in range(col_lo // PIECE,
                                min(npc, (col_hi + PIECE - 1) // PIECE)):
                    rs.append(wdeq_regions[(tag, ot, pc)])
            return rs

        def emit_ln(x_src_ap, x_reads, xb_lo, junk_lo, dst_lo, sidx, tag):
            JW = min(1024, D)
            npc = D // JW
            xb = A32(xb_lo, D)
            r_xb = R(xb_lo, 2 * D)
            r_junk = R(junk_lo, 2 * JW)
            r_st = p.region("stats", sidx * 16, sidx * 16 + 16)
            st = stats[:, sidx * 16:sidx * 16 + 16]
            r_dst = R(dst_lo, D)
            ld = dma("qs", xb[:, :], x_src_ap, list(x_reads), [r_xb], tag=tag)
            s1 = p.add("dve", lambda e: e.reduce_sum(st[:, 0:1], xb[:, :],
                                                     axis=AX.X),
                       reads=[r_xb], writes=[r_st], deps=[ld])
            p.add("dve", lambda e: e.tensor_scalar_mul(st[:, 1:2], st[:, 0:1],
                                                       -1.0 / D),
                  reads=[r_st], writes=[r_st], deps=[s1])
            sq = []
            for hf in range(npc):
                q1 = p.add(
                    "act", lambda e, hf=hf: e.activation(
                        out=A32(junk_lo, JW),
                        in_=xb[:, hf * JW:(hf + 1) * JW],
                        func=AF.Square, bias=st[:, 1:2], scale=1.0),
                    reads=[r_xb, r_st], writes=[r_junk], deps=[s1])
                sq.append(p.add(
                    "dve", lambda e, hf=hf: e.reduce_sum(
                        st[:, 2 + hf:3 + hf], A32(junk_lo, JW), axis=AX.X),
                    reads=[r_junk], writes=[r_st], deps=[q1]))
            acc = sq[0]
            for hf in range(1, npc):
                acc = p.add("dve", lambda e, hf=hf: e.tensor_add(
                    st[:, 2:3], st[:, 2:3], st[:, 2 + hf:3 + hf]),
                    reads=[r_st], writes=[r_st], deps=[acc, sq[hf]])
            s4 = p.add("act", lambda e: e.activation(out=st[:, 6:7],
                                                     in_=st[:, 2:3],
                                                     func=AF.Sqrt,
                                                     bias=csts[:, 0:1],
                                                     scale=1.0 / D),
                       reads=[r_st, r_csts], writes=[r_st], deps=[acc])
            s5 = p.add("dve", lambda e: e.reciprocal(st[:, 7:8], st[:, 6:7]),
                       reads=[r_st], writes=[r_st], deps=[s4])
            ln = p.add("dve", lambda e: e.tensor_scalar(
                out=A(dst_lo, D), in0=xb[:, :], scalar1=st[:, 1:2],
                scalar2=st[:, 7:8], op0=ALU.add, op1=ALU.mult),
                reads=[r_xb, r_st], writes=[r_dst], deps=[s5], tag=tag + "f")
            return ln, r_dst

        trsl = PSB[:, 1536:2048].bitcast(BF16)
        r_trsl = PR("PSB", 3)

        def emit_tr(src_lo, dst_lo, deps=(), scalar1=None, evac="dve",
                    slot=0, tag="tr"):
            r_src = R(src_lo, 128)
            r_dst = R(dst_lo, 128)
            mm = p.add("pe", lambda e: e.transpose(
                trsl[:, slot * 128:(slot + 1) * 128], A(src_lo, 128),
                ident[:, :]),
                reads=[r_src, r_ident], writes=[r_trsl], deps=list(deps),
                tag=tag)
            if scalar1 is None:
                if evac == "act":
                    fn = lambda e: e.copy(A(dst_lo, 128),
                                          trsl[:, slot * 128:(slot + 1) * 128])
                else:
                    fn = lambda e: e.tensor_copy(
                        A(dst_lo, 128), trsl[:, slot * 128:(slot + 1) * 128])
                ev = p.add(evac, fn, reads=[r_trsl], writes=[r_dst], deps=[mm],
                           tag=tag + "e")
            else:
                if evac == "act":
                    fn = lambda e: e.mul(A(dst_lo, 128),
                                         trsl[:, slot * 128:(slot + 1) * 128],
                                         scalar1)
                else:
                    fn = lambda e: e.tensor_scalar(
                        out=A(dst_lo, 128),
                        in0=trsl[:, slot * 128:(slot + 1) * 128],
                        scalar1=scalar1, scalar2=None, op0=ALU.mult)
                ev = p.add(evac, fn, reads=[r_trsl], writes=[r_dst], deps=[mm],
                           tag=tag + "e")
            return ev

        r_bvb = p.region("bvb")
        r_bvrow = p.region("bvrow")
        r_qkbT = p.region("qkbT")
        r_vbB = p.region("vbB")
        r_bind = p.region("bind_d")

        ldb = [dma("qs", bvb[:, i * HDA:(i + 1) * HDA], src, [], [r_bvb],
                   tag=f"bv{i}")
               for i, src in enumerate((bvq, bvk))]

        BTMP = 80640

        def bind_seq(col, sidx):
            src = bvb[:, col:col + HDA]
            r_c = p.region("bvb", col, col + HDA)
            st = stats[:, 16 + sidx * 4:20 + sidx * 4]
            r_st = p.region("stats", 16 + sidx * 4, 20 + sidx * 4)
            r_tmp = R(BTMP, HDA)
            a1 = p.add("dve", lambda e: e.reduce_sum(
                st[0:NHA, 0:1], src, axis=AX.X, apply_absolute_value=True),
                reads=[r_c], writes=[r_st], deps=list(ldb))
            a2 = p.add("dve", lambda e: e.tensor_scalar_mul(
                st[0:NHA, 1:2], st[0:NHA, 0:1], 1.0 / HDA),
                reads=[r_st], writes=[r_st], deps=[a1])
            sg = p.add("act", lambda e: e.activation(out=src, in_=src,
                                                     func=AF.Sign),
                       reads=[r_c], writes=[r_c], deps=[a1])
            t1 = p.add("act", lambda e: e.activation(
                out=A(BTMP, HDA)[0:NHA, :], in_=src, func=AF.Square),
                reads=[r_c], writes=[r_tmp], deps=[sg])
            t2 = p.add("dve", lambda e: e.tensor_sub(
                src, src, A(BTMP, HDA)[0:NHA, :]),
                reads=[r_c, r_tmp], writes=[r_c], deps=[t1])
            t3 = p.add("dve", lambda e: e.tensor_scalar_add(src, src, 1.0),
                       reads=[r_c], writes=[r_c], deps=[t2])
            fin = p.add("dve", lambda e: e.tensor_scalar(
                out=src, in0=src, scalar1=st[0:NHA, 1:2], scalar2=None,
                op0=ALU.mult),
                reads=[r_c, r_st], writes=[r_c], deps=[t3, a2])
            return fin

        bq = bind_seq(0, 0)
        bk = bind_seq(HDA, 1)
        qkb = p.add("dve", lambda e: e.tensor_mul(
            bvb[:, 0:HDA], bvb[:, 0:HDA], bvb[:, HDA:2 * HDA]),
            reads=[r_bvb], writes=[p.region("bvb", 0, HDA)], deps=[bq, bk])
        st_qkb = dma("qs", bind_d[0], bvb[0:NHA, 0:HDA], [r_bvb], [r_bind],
                     deps=[qkb], tag="qkbst")
        ldb.append(dma("qs", bvb[:, HDA:2 * HDA], bvv, [r_bvb],
                       [p.region("bvb", HDA, 2 * HDA)], deps=[qkb],
                       tag="bv2"))
        bv_ = bind_seq(HDA, 2)
        st_vb = dma("qs", bind_d[1], bvb[0:NHA, HDA:2 * HDA], [r_bvb],
                    [r_bind], deps=[bv_], tag="vbst")

        QKF = 80128
        r_qkf = R(QKF, 256)
        z0 = p.add("dve", lambda e: e.memset(A32(QKF, 128), 0.0),
                   writes=[r_qkf])
        ld_qk = dma("qs", A32(QKF, 128)[0:KD, 0:128],
                    bind_d[0].rearrange("h a -> (h a)").rearrange(
                        "(c p) -> c p", p=128),
                    [r_bind], [r_qkf], deps=[st_qkb, z0], tag="qkbld")
        ps_qk = PSB[:, 1024:1536]
        r_psqk = PR("PSB", 2)
        tr_qk = p.add("pe", lambda e: e.transpose(
            ps_qk[:, 0:128], A32(QKF, 128), identf[:, :]),
            reads=[r_qkf, r_identf], writes=[r_psqk], deps=[ld_qk])
        ev_qk = p.add("dve", lambda e: e.tensor_copy(qkbT[:, :],
                                                     ps_qk[:, 0:KD]),
                      reads=[r_psqk], writes=[r_qkbT], deps=[tr_qk])

        XB0, XB1, JUNK, LNS = 51456, 59648, 67840, 76032
        r_ln1 = [p.region("ln1_d", g, g + 1) for g in range(NSB)]
        r_lnq = [p.region("lnq_d", k, k + 1) for k in range(NLT)]
        for i in range(NSB + NLT):
            if i < NSB:
                src = x_full[i * 128:(i + 1) * 128, :]
                x_reads = []
            else:
                k = i - NSB
                src = x_my[k * 128:(k + 1) * 128, :]
                x_reads = []
            ln_op, r_lt = emit_ln(src, x_reads, XB0 if i % 2 == 0 else XB1,
                                  JUNK, LNS, 2 + i % 2, tag=f"lnA{i}")
            if i < NSB:
                dma("qa", ln1_d[i * 128:(i + 1) * 128, :], A(LNS, D),
                    [r_lt], [r_ln1[i]], deps=[ln_op], tag=f"l1s{i}")
            else:
                dma("qa", lnq_d[(i - NSB) * 128:(i - NSB + 1) * 128, :],
                    A(LNS, D), [r_lt], [r_lnq[i - NSB]], deps=[ln_op],
                    tag=f"lqs{i}")

        LHB, KSB, QTH, PTA = 0, 16384, 32768, 40960
        YHT, YBLK, LQB, MSK, SIGT = 45056, 47104, 49152, 50176, 50688

        def xres_reg(q, c0, c1):
            return p.region("x_res", q * D + c0, q * D + c1)

        for h in range(NHA):
            ldv = dma("qs", bvrow[0:1, 0:HDA], bind_d[1][h], [r_bind],
                      [r_bvrow], deps=[st_vb], tag=f"vbh{h}")
            cvv = p.add("dve", lambda e: e.tensor_copy(
                bvrow2[0:1, 0:HDA], bvrow[0:1, 0:HDA]),
                reads=[r_bvrow], writes=[p.region("bvrow2")], deps=[ldv])
            vb_evs = []
            VW = min(512, HDA)
            for hf in range((HDA + VW - 1) // VW):
                mm = p.add("pe", lambda e, hf=hf: e.matmul(
                    PSB[:, 0:VW], ones1[:, :],
                    bvrow2[0:1, hf * VW:(hf + 1) * VW],
                    start=True, stop=True),
                    reads=[p.region("bvrow2"), r_ones], writes=[PR("PSB", 0)],
                    deps=[cvv])
                vb_evs.append(p.add("act", lambda e, hf=hf: e.copy(
                    vbB[:, hf * VW:(hf + 1) * VW], PSB[:, 0:VW]),
                    reads=[PR("PSB", 0)], writes=[r_vbB], deps=[mm]))

            r_lhb = [R(LHB + g * HDA, HDA) for g in range(NSB)]
            lhb_ld = [dma("qs", A(LHB + g * HDA, HDA),
                          ln1_d[g * 128:(g + 1) * 128, h * HDA:(h + 1) * HDA],
                          [r_ln1[g]], [r_lhb[g]], tag=f"lh{h}.{g}")
                      for g in range(NSB)]
            k_ev = {}
            for g in range(NSB):
                for dc in range(KA):
                    k_ev[(g, dc)] = emit_tr(
                        LHB + g * HDA + dc * 128, KSB + dc * (NSB * 128) + g * 128,
                        deps=[lhb_ld[g]], slot=(g * KA + dc) % 4,
                        evac="act" if (g + dc) % 2 else "dve",
                        tag=f"K{h}.{g}.{dc}")
            v_op = {}
            for g in range(NSB):
                deps = [k_ev[(g, dc)] for dc in range(KA)] + vb_evs
                v_op[g] = p.add("dve", lambda e, g=g: e.tensor_mul(
                    A(LHB + g * HDA, HDA), A(LHB + g * HDA, HDA), vbB[:, 0:HDA]),
                    reads=[r_lhb[g], r_vbB], writes=[r_lhb[g]], deps=deps,
                    tag=f"V{h}.{g}")

            q_ev = {}
            for k in range(NLT):
                ldq2 = dma("qa", A(LQB, HDA),
                           lnq_d[k * 128:(k + 1) * 128, h * HDA:(h + 1) * HDA],
                           [r_lnq[k]], [R(LQB, HDA)], tag=f"lq{h}.{k}")
                for dc in range(KA):
                    gc = h * KA + dc
                    q_ev[(k, dc)] = emit_tr(
                        LQB + dc * 128, QTH + dc * TL + k * 128,
                        deps=[ldq2, ev_qk], scalar1=qkbT[:, gc:gc + 1],
                        slot=(k + dc) % 4, tag=f"Q{h}.{k}.{dc}")

            for m in range(NM):
                E = E256[m]
                pt_op = {}
                for j in range(E):
                    bank = j % 2
                    ps = PSA[:, bank * 512:bank * 512 + 256]
                    r_ps = PR("PSA", bank)
                    mm = None
                    for dc in range(KA):
                        mm = p.add("pe", lambda e, ps=ps, j=j, dc=dc, m=m:
                                   e.matmul(ps,
                                            A(KSB + dc * NSB * 128 + j * 128, 128),
                                            A(QTH + dc * TL + m * 256, 256),
                                            start=(dc == 0), stop=(dc == KA - 1)),
                                   reads=[R(KSB + dc * NSB * 128 + j * 128, 128),
                                          R(QTH + dc * TL + m * 256, 256)],
                                   writes=[r_ps],
                                   deps=[k_ev[(j, dc)], q_ev[(2 * m, dc)],
                                         q_ev[(2 * m + 1, dc)]],
                                   tag=f"sa{h}.{m}.{j}")
                    sg2 = p.add("act", lambda e, ps=ps: e.activation(
                        out=A(SIGT, 256), in_=ps, func=AF.Sigmoid, scale=0.125),
                        reads=[r_ps], writes=[R(SIGT, 256)], deps=[mm],
                        tag=f"sg{h}.{m}.{j}")
                    msl = MSK + (j % 2) * 256
                    mld = dma("qa", A(msl, 256), mska[m, j], [],
                              [R(msl, 256)], tag=f"ma{m}.{j}")
                    pt_op[j] = p.add("dve", lambda e, j=j, msl=msl: e.tensor_mul(
                        A(PTA + j * 256, 256), A(SIGT, 256), A(msl, 256)),
                        reads=[R(SIGT, 256), R(msl, 256)],
                        writes=[R(PTA + j * 256, 256)], deps=[sg2, mld],
                        tag=f"pt{h}.{m}.{j}")
                ytr = {}
                for dc in range(KA):
                    ps = PSB[:, 512:768]
                    r_ps = PR("PSB", 1)
                    mm = None
                    for j in range(E):
                        mm = p.add("pe", lambda e, ps=ps, j=j, dc=dc, E=E: e.matmul(
                            ps, A(LHB + j * HDA + dc * 128, 128),
                            A(PTA + j * 256, 256),
                            start=(j == 0), stop=(j == E - 1)),
                            reads=[r_lhb[j], R(PTA + j * 256, 256)],
                            writes=[r_ps], deps=[v_op[j], pt_op[j]],
                            tag=f"oa{h}.{m}.{dc}")
                    ev = p.add("act", lambda e, ps=ps, dc=dc: e.copy(
                        A(YHT + dc * 256, 256), ps),
                        reads=[r_ps], writes=[R(YHT + dc * 256, 256)],
                        deps=[mm])
                    for ts in range(2):
                        ytr[(dc, ts)] = emit_tr(
                            YHT + dc * 256 + ts * 128,
                            YBLK + ts * 1024 + dc * 128,
                            deps=[ev], slot=(dc + ts) % 4,
                            tag=f"yb{h}.{m}.{dc}.{ts}")
                for ts in range(2):
                    q = 2 * m + ts
                    deps = [ytr[(dc, ts)] for dc in range(KA)]
                    xb2 = A32(XB0, 2048)
                    r_xb2 = R(XB0, 4096)
                    ld = dma("qs", xb2[:, 0:HDA],
                             x_my[q * 128:(q + 1) * 128, h * HDA:(h + 1) * HDA],
                             [], [r_xb2], deps=deps, tag=f"xa{h}.{q}")
                    ad = p.add("dve", lambda e, ts=ts: e.tensor_add(
                        xb2[:, 0:HDA], xb2[:, 0:HDA], A(YBLK + ts * 1024, HDA)),
                        reads=[r_xb2, R(YBLK + ts * 1024, HDA)],
                        writes=[r_xb2], deps=[ld] + deps)
                    dma("qs", x_res[q * 128:(q + 1) * 128,
                                    h * HDA:(h + 1) * HDA],
                        xb2[:, 0:HDA], [r_xb2],
                        [xres_reg(q, h * HDA, (h + 1) * HDA)], deps=[ad],
                        tag=f"xs{h}.{q}")

        def emit_ffn(x_src, x_dst, dst_name, phase):
            LNT, ACTT = 0, 16384
            SG, ATOK = 60416, 64512
            WM0 = 68608
            XS = 70656
            LXB, LJUNK = 72704, 80896
            LLN = 60416
            HBW = 1024
            nhb = (H + HBW - 1) // HBW
            for th in range(2):
                for kt in range(KT):
                    q = th * KT + kt
                    ln_op, r_lt = emit_ln(
                        x_src[q * 128:(q + 1) * 128, :],
                        [p.region("x_res", q * D, (q + 1) * D)],
                        LXB, LJUNK, LLN, 2 + kt % 2, tag=f"lnF{phase}.{q}")
                    for c in range(KD):
                        emit_tr(LLN + c * 128, LNT + c * TH + kt * 128,
                                deps=[ln_op], slot=c % 4,
                                evac="act" if c % 2 else "dve",
                                tag=f"lt{phase}.{q}.{c}")
                for hb in range(nhb):
                    w = min(HBW, H - hb * HBW)
                    nsub = (w + 511) // 512
                    mm_last = {}
                    for tag_m, wsrc in (("g", wg_d), ("u", wu_d)):
                        for c in range(KD):
                            wml = WM0 + (c % 2) * HBW
                            r_wm = R(wml, HBW)
                            ldw = dmaT("qs", A(wml, w),
                                       wsrc[hb * HBW:hb * HBW + w,
                                            c * 128:(c + 1) * 128],
                                       wsrc_regions(tag_m, hb * HBW,
                                                    hb * HBW + w, c * 128,
                                                    (c + 1) * 128, D),
                                       [r_wm],
                                       tag=f"W{tag_m}{phase}.{th}.{hb}.{c}")
                            for sub in range(nsub):
                                sw = min(512, w - sub * 512)
                                ps_ap = PSA if sub == 0 else PSB
                                psn = "PSA" if sub == 0 else "PSB"
                                for kt in range(KT):
                                    ps = ps_ap[:, kt * 512:kt * 512 + sw]
                                    mm_last[(tag_m, sub, kt)] = p.add(
                                        "pe",
                                        lambda e, ps=ps, c=c, kt=kt, wml=wml,
                                        sub=sub, sw=sw:
                                        e.matmul(ps,
                                                 A(LNT + c * TH + kt * 128, 128),
                                                 A(wml + sub * 512, sw),
                                                 start=(c == 0),
                                                 stop=(c == KD - 1)),
                                        reads=[R(LNT + c * TH + kt * 128, 128),
                                               r_wm],
                                        writes=[PR(psn, kt)], deps=[ldw],
                                        tag=f"M{tag_m}{phase}.{th}.{hb}.{c}.{sub}.{kt}")
                        if tag_m == "g":
                            for sub in range(nsub):
                                sw = min(512, w - sub * 512)
                                ps_ap = PSA if sub == 0 else PSB
                                psn = "PSA" if sub == 0 else "PSB"
                                for kt in range(KT):
                                    mm_last[("sg", sub, kt)] = p.add(
                                        "act",
                                        lambda e, kt=kt, sub=sub, sw=sw,
                                        ps_ap=ps_ap: e.activation(
                                            out=A(SG + kt * HBW + sub * 512, sw),
                                            in_=ps_ap[:, kt * 512:kt * 512 + sw],
                                            func=AF.Silu),
                                        reads=[PR(psn, kt)],
                                        writes=[R(SG + kt * HBW + sub * 512, sw)],
                                        deps=[mm_last[("g", sub, kt)]],
                                        tag=f"si{phase}.{th}.{hb}.{sub}.{kt}")
                    at_ops = {}
                    for sub in range(nsub):
                        sw = min(512, w - sub * 512)
                        ps_ap = PSA if sub == 0 else PSB
                        psn = "PSA" if sub == 0 else "PSB"
                        for kt in range(KT):
                            at_ops[(sub, kt)] = p.add(
                                "dve", lambda e, kt=kt, sub=sub, sw=sw,
                                ps_ap=ps_ap: e.tensor_mul(
                                    A(ATOK + kt * HBW + sub * 512, sw),
                                    A(SG + kt * HBW + sub * 512, sw),
                                    ps_ap[:, kt * 512:kt * 512 + sw]),
                                reads=[R(SG + kt * HBW + sub * 512, sw),
                                       PR(psn, kt)],
                                writes=[R(ATOK + kt * HBW + sub * 512, sw)],
                                deps=[mm_last[("sg", sub, kt)],
                                      mm_last[("u", sub, kt)]],
                                tag=f"am{phase}.{th}.{hb}.{sub}.{kt}")
                    for kt in range(KT):
                        for hc in range(w // 128):
                            ch = hb * (HBW // 128) + hc
                            emit_tr(ATOK + kt * HBW + hc * 128,
                                    ACTT + ch * TH + kt * 128,
                                    deps=[at_ops[(hc // 4, kt)]],
                                    slot=(kt + hc) % 4,
                                    evac="act" if hc % 2 else "dve",
                                    tag=f"at{phase}.{th}.{hb}.{kt}.{hc}")
                DBW = min(1024, D)
                for db in range(D // DBW):
                    ndsub = DBW // 512
                    mm_last = {}
                    for c in range(KH):
                        wml = WM0 + (c % 2) * HBW
                        r_wm = R(wml, HBW)
                        ldw = dmaT("qs", A(wml, DBW),
                                   wd_d[db * DBW:(db + 1) * DBW,
                                        c * 128:(c + 1) * 128],
                                   wsrc_regions("d", db * DBW, (db + 1) * DBW,
                                                c * 128, (c + 1) * 128, H),
                                   [r_wm], tag=f"Wd{phase}.{th}.{db}.{c}")
                        for sub in range(ndsub):
                            ps_ap = PSA if sub == 0 else PSB
                            psn = "PSA" if sub == 0 else "PSB"
                            for kt in range(KT):
                                ps = ps_ap[:, kt * 512:(kt + 1) * 512]
                                mm_last[(sub, kt)] = p.add(
                                    "pe", lambda e, ps=ps, c=c, kt=kt, wml=wml,
                                    sub=sub:
                                    e.matmul(ps, A(ACTT + c * TH + kt * 128, 128),
                                             A(wml + sub * 512, 512),
                                             start=(c == 0),
                                             stop=(c == KH - 1)),
                                    reads=[R(ACTT + c * TH + kt * 128, 128),
                                           r_wm],
                                    writes=[PR(psn, kt)], deps=[ldw],
                                    tag=f"Md{phase}.{th}.{db}.{c}.{sub}.{kt}")
                    for sub in range(ndsub):
                        ps_ap = PSA if sub == 0 else PSB
                        psn = "PSA" if sub == 0 else "PSB"
                        for kt in range(KT):
                            q = th * KT + kt
                            col = db * DBW + sub * 512
                            sl = (kt + sub * KT) % 2
                            xs = A32(XS + sl * 1024, 512)
                            r_xs = R(XS + sl * 1024, 1024)
                            ld = dma("qs", xs[:, 0:512],
                                     x_src[q * 128:(q + 1) * 128, col:col + 512],
                                     [p.region("x_res", q * D + col,
                                               q * D + col + 512)],
                                     [r_xs], tag=f"xf{phase}.{th}.{db}.{sub}.{kt}")
                            ad = p.add("dve", lambda e, kt=kt, ps_ap=ps_ap,
                                       xs=xs: e.tensor_add(
                                           xs[:, 0:512], xs[:, 0:512],
                                           ps_ap[:, kt * 512:(kt + 1) * 512]),
                                       reads=[r_xs, PR(psn, kt)], writes=[r_xs],
                                       deps=[ld, mm_last[(sub, kt)]])
                            dma("qs", x_dst[q * 128:(q + 1) * 128, col:col + 512],
                                xs[:, 0:512], [r_xs],
                                [p.region(dst_name, q * D + col,
                                          q * D + col + 512)],
                                deps=[ad], tag=f"xo{phase}.{th}.{db}.{sub}.{kt}")

        if 2 in PHASES:
            emit_ffn(x_res, x_res, "x_res", 2)

        if 3 in PHASES:
            LNT3, QT3 = 0, 32768
            OUTT = 0
            WQF, WQB = 65536, 73728
            KSB3, VSB3, KVT = 77824, 79872, 81920
            PB3, PTB3, MB3, XS3 = 86016, 88064, 90112, 94208
            L3XB, L3JK, L3LN = 65536, 73728, 81920

            for kt in range(NLT):
                ln_op, _ = emit_ln(x_res[kt * 128:(kt + 1) * 128, :],
                                   [p.region("x_res", kt * D, (kt + 1) * D)],
                                   L3XB, L3JK, L3LN, 2 + kt % 2, tag=f"ln3.{kt}")
                for c in range(KD):
                    emit_tr(L3LN + c * 128, LNT3 + c * TL + kt * 128,
                            deps=[ln_op], slot=c % 4,
                            evac="act" if c % 2 else "dve", tag=f"l3t{kt}.{c}")

            def proj(w_in, col0, dst_lo, tagp):
                ldf = dma("qa", A32(WQF, KD * 128).rearrange("p (c n) -> p c n", c=KD),
                          w_in.rearrange("(c p) n -> p c n", p=128)[:, :,
                                                                   col0:col0 + 128],
                          [], [R(WQF, 2 * KD * 128)], tag=f"pw{tagp}")
                cv = p.add("dve", lambda e: e.tensor_copy(A(WQB, KD * 128),
                                                          A32(WQF, KD * 128)),
                           reads=[R(WQF, 2 * KD * 128)], writes=[R(WQB, KD * 128)],
                           deps=[ldf], tag=f"pc{tagp}")
                evs = []
                PW = min(512, TL)
                for thl in range(TL // PW):
                    mm = None
                    for c in range(KD):
                        mm = p.add("pe", lambda e, c=c, thl=thl: e.matmul(
                            PSB[:, 0:PW], A(WQB + c * 128, 128),
                            A(LNT3 + c * TL + thl * PW, PW),
                            start=(c == 0), stop=(c == KD - 1)),
                            reads=[R(WQB + c * 128, 128),
                                   R(LNT3 + c * TL + thl * PW, PW)],
                            writes=[PR("PSB", 0)], deps=[cv], tag=f"pm{tagp}.{thl}")
                    evs.append(p.add("act", lambda e, thl=thl: e.copy(
                        A(dst_lo + thl * PW, PW), PSB[:, 0:PW]),
                        reads=[PR("PSB", 0)], writes=[R(dst_lo + thl * PW, PW)],
                        deps=[mm], tag=f"pe{tagp}.{thl}"))
                return evs

            q_ev3 = {hq: proj(wq, hq * 128, QT3 + hq * TL, f"q{hq}")
                     for hq in range(NHB)}
            k_ev3 = proj(wk, 0, KSB3, "k")
            v_ev3 = proj(wv, 0, VSB3, "v")

            kv_tr = []
            for slot, src_lo in ((0, KSB3), (1, VSB3)):
                for tt in range(NLT):
                    kv_tr.append(emit_tr(src_lo + tt * 128,
                                         KVT + slot * (NLT * 128) + tt * 128,
                                         deps=(k_ev3 if slot == 0 else v_ev3),
                                         slot=tt % 4, tag=f"kv{slot}.{tt}"))

            r_kvW = p.region("kv_shW")
            r_kvt = R(KVT, 2 * NLT * 128)
            ladder_n = cfg["ladder"]

            def _kv_exchange(e):
                pid = e.partition_id()
                par = e.snap(pid % 2, min_val=0, max_val=1)
                kv4 = kv_sh
                for slot in range(2):
                    e.dma_start(
                        out=kv4[e.snap(par * 2 + slot, min_val=0, max_val=3)]
                        .rearrange("(tt pp) d -> pp tt d", pp=128),
                        in_=A(KVT + slot * NLT * 128, NLT * 128)
                        .rearrange("pp (tt d) -> pp tt d", tt=NLT),
                    ).then_inc(s_lad, 16)
                    e.wait_ge(s_lad, 16 * (slot + 1))
                e.dma_start(out=fl_sh[:, bass.ds(par * 4, 4)],
                            in_=flone[0:2, 0:4]).then_inc(s_lad, 16)
                e.wait_ge(s_lad, 48)
                with (
                    e.register("f0") as f0,
                    e.register("f1") as f1,
                    e.register("tg") as tg,
                    e.register("ptgt") as ptgt,
                    e.register("zr") as zr,
                ):
                    e.reg_mov(f0, 0)
                    e.reg_mov(f1, 0)
                    e.reg_load(tg, flone[0:1, 0:1])
                    e.reg_mov(ptgt, 48)
                    e.reg_alu(zr, par, 1, ALU.logical_shift_right)
                    zidx = e.snap(zr, min_val=0, max_val=1)
                    for _ in range(ladder_n):
                        nd = e.compute_val((f0 != tg) | (f1 != tg))
                        e.dma_start(out=flsb[0:1, 0:8], in_=fl_sh[zidx][None, :],
                                    cond=nd).then_inc(s_lad, 16)
                        c16 = e.compute_val(nd * 16)
                        e.reg_alu(ptgt, ptgt, c16, ALU.add)
                        e.wait_ge(s_lad, ptgt)
                        e.reg_load(f0, flsb[0:1, 0:1])
                        e.reg_load(f1, flsb[0:1, 4:5])
                return e.nop()

            lad_done = p.add("pool", _kv_exchange,
                             reads=[r_kvt, r_flone, p.region("flsb")],
                             writes=[r_kvW, r_flW, p.region("flsb")],
                             deps=kv_tr, tag="kvex")

            vvld, kkld = {}, {}
            for g in range(NSB):
                w_, l_ = owner[g], lidx[g]
                vvld[g] = dma("qp", A(VSB3 + g * 128, 128),
                              kv_sh[2 * w_ + 1, l_ * 128:(l_ + 1) * 128, :],
                              [r_kvW], [R(VSB3 + g * 128, 128)], deps=[lad_done],
                              tag=f"vl{g}")
                ktmp = dma("qp", A(KVT + (g % 2) * 128, 128),
                           kv_sh[2 * w_, l_ * 128:(l_ + 1) * 128, :],
                           [r_kvW], [R(KVT + (g % 2) * 128, 128)],
                           deps=[lad_done], tag=f"kl{g}")
                kkld[g] = emit_tr(KVT + (g % 2) * 128, KSB3 + g * 128,
                                  deps=[ktmp], slot=g % 4, tag=f"kt3{g}")

            mb_ld = [dma("qa", A(MB3 + k * 512, 512), mskb[k], [],
                         [R(MB3 + k * 512, 512)], tag=f"mb{k}")
                     for k in range(NLT)]

            ot_ev = {}
            for h in range(NHB):
                for k in range(NLT):
                    E = EB[k]
                    mm = {}
                    for j in range(E):
                        mm[j] = p.add("pe", lambda e, h=h, k=k, j=j: e.matmul(
                            PSA[:, j * 512:(j + 1) * 512],
                            A(QT3 + h * TL + k * 128, 128),
                            A(KSB3 + j * 512, 512), start=True, stop=True),
                            reads=[R(QT3 + h * TL + k * 128, 128)] +
                            [R(KSB3 + g * 128, 128)
                             for g in range(4 * j, min(4 * j + 4, NSB))],
                            writes=[PR("PSA", j)],
                            deps=[q_ev3[h][(k * 128) // min(512, TL)]] +
                            [kkld[g] for g in range(4 * j, min(4 * j + 4, NSB))],
                            tag=f"sb{h}.{k}.{j}")
                    jb = E - 1
                    mka = p.add("dve", lambda e, jb=jb, k=k: e.tensor_add(
                        PSA[:, jb * 512:(jb + 1) * 512],
                        PSA[:, jb * 512:(jb + 1) * 512], A(MB3 + k * 512, 512)),
                        reads=[PR("PSA", jb), R(MB3 + k * 512, 512)],
                        writes=[PR("PSA", jb)], deps=[mm[jb], mb_ld[k]],
                        tag=f"mk{h}.{k}")
                    r_st = p.region("stats", 0, 8)
                    rmax = p.add("dve", lambda e, E=E: e.reduce_max(
                        stats[:, 0:1], PSA[:, 0:E * 512], axis=AX.X, negate=True),
                        reads=[PR("PSA", 0, E)], writes=[r_st],
                        deps=[mka] + list(mm.values()), tag=f"rm{h}.{k}")
                    rsc = p.add("dve", lambda e: e.tensor_scalar_mul(
                        stats[:, 0:1], stats[:, 0:1], float(128 ** -0.5)),
                        reads=[r_st], writes=[r_st], deps=[rmax])
                    exp_ = p.add("act", lambda e, E=E: e.activation(
                        out=A(PB3, E * 512), in_=PSA[:, 0:E * 512], func=AF.Exp,
                        bias=stats[:, 0:1], scale=float(128 ** -0.5),
                        accum_out=stats[:, 1:2]),
                        reads=[PR("PSA", 0, E), r_st], writes=[R(PB3, 2048), r_st],
                        deps=[rsc], tag=f"ex{h}.{k}")
                    rcp = p.add("dve", lambda e: e.reciprocal(stats[:, 2:3],
                                                              stats[:, 1:2]),
                                reads=[r_st], writes=[r_st], deps=[exp_])
                    pn = p.add("dve", lambda e, E=E: e.tensor_scalar(
                        out=A(PB3, E * 512), in0=A(PB3, E * 512),
                        scalar1=stats[:, 2:3], scalar2=None, op0=ALU.mult),
                        reads=[R(PB3, 2048), r_st], writes=[R(PB3, 2048)],
                        deps=[rcp], tag=f"pn{h}.{k}")
                    pt_t = {}
                    for sc in range(E * 4):
                        pt_t[sc] = emit_tr(PB3 + sc * 128, PTB3 + sc * 128,
                                           deps=[pn], slot=sc % 4,
                                           evac="act" if sc % 2 else "dve",
                                           tag=f"pb{h}.{k}.{sc}")
                    mmo = None
                    for sc in range(min(E * 4, NSB)):
                        mmo = p.add("pe", lambda e, sc=sc, E=E: e.matmul(
                            PSB[:, 512:640], A(VSB3 + sc * 128, 128),
                            A(PTB3 + sc * 128, 128), start=(sc == 0),
                            stop=(sc == min(E * 4, NSB) - 1)),
                            reads=[R(VSB3 + sc * 128, 128),
                                   R(PTB3 + sc * 128, 128)],
                            writes=[PR("PSB", 1)], deps=[vvld[sc], pt_t[sc]],
                            tag=f"ob{h}.{k}.{sc}")
                    ot_ev[(h, k)] = p.add("dve", lambda e, h=h, k=k: e.tensor_copy(
                        A(OUTT + h * TL + k * 128, 128), PSB[:, 512:640]),
                        reads=[PR("PSB", 1)],
                        writes=[R(OUTT + h * TL + k * 128, 128)], deps=[mmo],
                        tag=f"oe{h}.{k}")

            for db in range(D // 512):
                mm_last = {}
                for c in range(NHB):
                    sl = c % 2
                    wml = WQB + sl * 512
                    r_wm = R(wml, 512)
                    ldf = dma("qa", A32(WQF + sl * 1024, 512),
                              wo[c * 128:(c + 1) * 128, db * 512:(db + 1) * 512],
                              [], [R(WQF + sl * 1024, 1024)], tag=f"wo{db}.{c}")
                    cvw = p.add("dve", lambda e, wml=wml, sl=sl: e.tensor_copy(
                        A(wml, 512), A32(WQF + sl * 1024, 512)),
                        reads=[R(WQF + sl * 1024, 1024)], writes=[r_wm],
                        deps=[ldf], tag=f"wc{db}.{c}")
                    for q in range(NLT):
                        psn, ps_ap = ("PSA", PSA) if q < 4 else ("PSB", PSB)
                        bb = q % 4
                        ps = ps_ap[:, bb * 512:(bb + 1) * 512]
                        mm_last[q] = p.add(
                            "pe", lambda e, ps=ps, c=c, q=q, wml=wml: e.matmul(
                                ps, A(OUTT + c * TL + q * 128, 128), A(wml, 512),
                                start=(c == 0), stop=(c == NHB - 1)),
                            reads=[R(OUTT + c * TL + q * 128, 128), r_wm],
                            writes=[PR(psn, bb)], deps=[cvw, ot_ev[(c, q)]],
                            tag=f"wm{db}.{c}.{q}")
                for q in range(NLT):
                    psn, ps_ap = ("PSA", PSA) if q < 4 else ("PSB", PSB)
                    bb = q % 4
                    xs = A32(XS3 + (q % 2) * 1024, 512)
                    r_xs = R(XS3 + (q % 2) * 1024, 1024)
                    ld = dma("qs", xs[:, 0:512],
                             x_res[q * 128:(q + 1) * 128, db * 512:(db + 1) * 512],
                             [p.region("x_res", q * D + db * 512,
                                       q * D + (db + 1) * 512)],
                             [r_xs], tag=f"x3{db}.{q}")
                    ad = p.add("dve", lambda e, ps_ap=ps_ap, bb=bb, xs=xs: e.tensor_add(
                        xs[:, 0:512], xs[:, 0:512],
                        ps_ap[:, bb * 512:(bb + 1) * 512]),
                        reads=[r_xs, PR(psn, bb)], writes=[r_xs],
                        deps=[ld, mm_last[q]])
                    dma("qs", x_res[q * 128:(q + 1) * 128,
                                    db * 512:(db + 1) * 512],
                        xs[:, 0:512], [r_xs],
                        [p.region("x_res", q * D + db * 512,
                                  q * D + (db + 1) * 512)],
                        deps=[ad], tag=f"x3o{db}.{q}")

        if DEBUG_XRES:
            DBB = 91136
            for q in range(NLT):
                xsv = p.region("x_res", q * D, (q + 1) * D)
                tmp = R(DBB, 2048)
                for hf2 in range(2):
                    l1 = dma("qs", A32(DBB, 1024)[:, 0:D // 2],
                             x_res[q * 128:(q + 1) * 128,
                                   hf2 * (D // 2):(hf2 + 1) * (D // 2)],
                             [xsv], [tmp], tag=f"xd{q}.{hf2}a")
                    dma("qs", xdbg[q * 128:(q + 1) * 128,
                                   hf2 * (D // 2):(hf2 + 1) * (D // 2)],
                        A32(DBB, 1024)[:, 0:D // 2],
                        [tmp], [p.region("xdbg", q, q + 1)], deps=[l1],
                        tag=f"xd{q}.{hf2}b")

        if 4 in PHASES:
            emit_ffn(x_res, out, "out", 4)
        else:
            CPB = 91136
            for q in range(NLT):
                xsv = p.region("x_res", q * D, (q + 1) * D)
                tmp = R(CPB, 2048)
                l1 = dma("qs", A32(CPB, 1024)[:, 0:D // 2],
                         x_res[q * 128:(q + 1) * 128, 0:D // 2],
                         [xsv], [tmp], tag=f"cpo{q}a")
                dma("qs", out[q * 128:(q + 1) * 128, 0:D // 2],
                    A32(CPB, 1024)[:, 0:D // 2],
                    [tmp], [p.region("out", q * D, (q + 1) * D)], deps=[l1],
                    tag=f"cpo{q}b")
                l2 = dma("qs", A32(CPB, 1024)[:, 0:D // 2],
                         x_res[q * 128:(q + 1) * 128, D // 2:D],
                         [xsv], [tmp], tag=f"cpo{q}c")
                dma("qs", out[q * 128:(q + 1) * 128, D // 2:D],
                    A32(CPB, 1024)[:, 0:D // 2],
                    [tmp], [p.region("out", q * D, (q + 1) * D)], deps=[l2],
                    tag=f"cpo{q}d")

        section = p.emit_all(sems)

        @block.tensor
        def _(te):
            section(te, "tensor")

        @block.vector
        def _(ve):
            section(ve, "vector")

        @block.scalar
        def _(sc):
            section(sc, "scalar")

        @block.gpsimd
        def _(gp):
            section(gp, "gpsimd")

        @block.sync
        def _(sy):
            section(sy, "sync")

    return nc




def stage_inputs(cfg, inputs):
    x = np.ascontiguousarray(np.asarray(inputs["L0_attn_a_x"], dtype=np.float32))
    qgT = _int8_T(inputs["L0_ffn_qw_gate"])
    quT = _int8_T(inputs["L0_ffn_qw_up"])
    qdT = _int8_T(inputs["L0_ffn_qw_down"])
    zgT = np.ascontiguousarray(np.asarray(inputs["L0_ffn_z_gate"]).T.astype(np.int32))
    sgT = np.ascontiguousarray(np.asarray(inputs["L0_ffn_s_gate"]).T.astype(np.float32))
    zuT = np.ascontiguousarray(np.asarray(inputs["L0_ffn_z_up"]).T.astype(np.int32))
    suT = np.ascontiguousarray(np.asarray(inputs["L0_ffn_s_up"]).T.astype(np.float32))
    zdT = np.ascontiguousarray(np.asarray(inputs["L0_ffn_z_down"]).T.astype(np.int32))
    sdT = np.ascontiguousarray(np.asarray(inputs["L0_ffn_s_down"]).T.astype(np.float32))
    com = dict(
        qgT=qgT, quT=quT, qdT=qdT, zgT=zgT, sgT=sgT, zuT=zuT, suT=suT,
        zdT=zdT, sdT=sdT,
        bvq=np.asarray(inputs["L0_attn_a_bv_q"], dtype=np.float32),
        bvk=np.asarray(inputs["L0_attn_a_bv_k"], dtype=np.float32),
        bvv=np.asarray(inputs["L0_attn_a_bv_v"], dtype=np.float32),
        wq=np.asarray(inputs["L0_attn_b_wq"], dtype=np.float32),
        wk=np.asarray(inputs["L0_attn_b_wk"], dtype=np.float32),
        wv=np.asarray(inputs["L0_attn_b_wv"], dtype=np.float32),
        wo=np.asarray(inputs["L0_attn_b_wo"], dtype=np.float32),
        identity=np.eye(128, dtype=BF),
        runtag=np.full((2, 4), np.random.randint(1, 2**30), dtype=np.int32),
    )
    masks = [(_masks_a(cfg, par), _masks_b(cfg, par)) for par in (0, 1)]
    in_maps = []
    for c in range(2 * cfg["B"]):
        b, par = c // 2, c % 2
        rows = np.concatenate([np.arange(g * 128, (g + 1) * 128)
                               for g in cfg["sets"][par]])
        m = dict(com)
        m["x_my"] = np.ascontiguousarray(x[b][rows])
        m["x_full"] = np.ascontiguousarray(x[b])
        m["mska"], m["mskb"] = masks[par]
        in_maps.append(m)
    return in_maps


def gather_output(cfg, results):
    B, T, D = cfg["B"], cfg["T"], cfg["D"]
    out = np.empty((B, T, D), dtype=np.float32)
    for b in range(B):
        for g in range(cfg["NQT"]):
            par = cfg["owner"][g]
            li = cfg["lidx"][g]
            out[b, g * 128:(g + 1) * 128] = \
                results[2 * b + par]["out"][li * 128:(li + 1) * 128]
    return out


_NC_CACHE = {}


def kernel(**inputs):
    cfg = CFG
    key = "main"
    if key not in _NC_CACHE:
        _NC_CACHE[key] = build_program(cfg)
    nc = _NC_CACHE[key]
    in_maps = stage_inputs(cfg, inputs)
    res = run_bass_kernel_spmd(nc, in_maps, core_ids=list(range(2 * cfg["B"])))
    return gather_output(cfg, res.results)


# version 05 (final)
# speedup vs baseline: 1.9491x; 1.1626x over previous
import numpy as np
import ml_dtypes
import concourse.bass as bass
import concourse.mybir as mybir
from concourse.bass_utils import run_bass_kernel_spmd

F32 = mybir.dt.float32
BF16 = mybir.dt.bfloat16
I8 = mybir.dt.int8
I32 = mybir.dt.int32
AF = mybir.ActivationFunctionType
ALU = mybir.AluOpType
AX = mybir.AxisListType
BF = ml_dtypes.bfloat16


def make_cfg(D=4096, H=11008, T=2048, B=4, NHA=4, NHB=32, ladder=768):
    cfg = dict(D=D, H=H, T=T, B=B, NHA=NHA, NHB=NHB, HDB=128, ladder=ladder)
    cfg["HDA"] = D // NHA
    cfg["KD"] = D // 128
    cfg["KH"] = H // 128
    cfg["NQT"] = T // 128
    cfg["TL"] = T // 2
    cfg["NLT"] = cfg["NQT"] // 2
    assert cfg["HDA"] % 128 == 0 and NHB * 128 == D and H % 128 == 0
    owner = [0 if (i % 4) in (0, 3) else 1 for i in range(cfg["NQT"])]
    sets = [[i for i in range(cfg["NQT"]) if owner[i] == p] for p in (0, 1)]
    cfg["owner"], cfg["sets"] = owner, sets
    cfg["lidx"] = {g: li for p in (0, 1) for li, g in enumerate(sets[p])}
    nlt = cfg["NLT"]
    cfg["E256"] = [max(sets[0][2 * m + 1], sets[1][2 * m + 1]) + 1
                   for m in range(nlt // 2)]
    cfg["EB"] = [(max(sets[0][k], sets[1][k]) * 128 + 128 + 511) // 512
                 for k in range(nlt)]
    return cfg


CFG = make_cfg()
PHASES = {0, 1, 2, 3, 4}
DEBUG_XRES = False



def _int8_T(qw_int32):
    t = np.ascontiguousarray(np.asarray(qw_int32).T)
    return np.ascontiguousarray(t.view(np.int8)[:, 0::4])


def _masks_a(cfg, parity):
    L = cfg["sets"][parity]
    n_m = cfg["NLT"] // 2
    out = np.zeros((max(n_m, 1), cfg["NQT"], 128, 256), dtype=BF)
    ss = np.arange(128)
    for m in range(n_m):
        for j in range(cfg["E256"][m]):
            blk = np.zeros((128, 256), dtype=np.float32)
            for sub in range(2):
                tglob = L[2 * m + sub] * 128 + np.arange(128)
                blk[:, sub * 128:(sub + 1) * 128] = (
                    (j * 128 + ss)[:, None] <= tglob[None, :]).astype(np.float32)
            out[m, j] = blk.astype(BF)
    return out


def _masks_b(cfg, parity):
    L = cfg["sets"][parity]
    out = np.zeros((cfg["NLT"], 128, 512), dtype=BF)
    for k in range(cfg["NLT"]):
        j = cfg["EB"][k] - 1
        tglob = L[k] * 128 + np.arange(128)
        sglob = j * 512 + np.arange(512)
        out[k] = np.where(sglob[None, :] <= tglob[:, None], 0.0, -1e9).astype(BF)
    return out



STREAMS = ("pe", "dve", "act", "pool", "qs", "qa", "qp")
STREAM_ENGINE = {"pe": "tensor", "dve": "vector", "act": "scalar",
                 "pool": "gpsimd", "qs": "sync", "qa": "scalar", "qp": "gpsimd"}
ENGINES = ("tensor", "vector", "scalar", "gpsimd", "sync")
DMA_STREAMS = ("qs", "qa", "qp")


class Op:
    __slots__ = ("idx", "stream", "emit", "deps", "inc", "seq", "tag")

    def __init__(self, idx, stream, emit, tag):
        self.idx, self.stream, self.emit, self.tag = idx, stream, emit, tag
        self.deps = []
        self.inc = stream in DMA_STREAMS
        self.seq = None


class Region:
    __slots__ = ("writer", "readers", "overlaps")

    def __init__(self):
        self.writer = None
        self.readers = []
        self.overlaps = []


class Prog:
    def __init__(self):
        self.ops = []
        self.regions = {}
        self.stash = {}

    def region(self, tensor, lo=0, hi=1 << 60):
        key = (tensor, lo, hi)
        r = self.regions.get(key)
        if r is None:
            r = Region()
            for (t2, l2, h2), r2 in self.regions.items():
                if t2 == tensor and not (hi <= l2 or h2 <= lo):
                    r.overlaps.append(r2)
                    r2.overlaps.append(r)
            self.regions[key] = r
        return r

    def add(self, stream, emit, reads=(), writes=(), deps=(), tag=""):
        me = Op(len(self.ops), stream, emit, tag)
        dset = me.deps

        def dep(op):
            if op is not None and op is not me and op not in dset:
                assert op.idx < me.idx
                dset.append(op)

        for d in deps:
            dep(d)
        for r in reads:
            dep(r.writer)
            for r2 in r.overlaps:
                dep(r2.writer)
        for r in writes:
            dep(r.writer)
            for rd in r.readers:
                dep(rd)
            for r2 in r.overlaps:
                dep(r2.writer)
                for rd in r2.readers:
                    dep(rd)
        self.ops.append(me)
        for r in reads:
            r.readers.append(me)
        for r in writes:
            r.writer = me
            r.readers = []
        return me

    NSUB = 4

    def emit_all(self, sems):
        for op in self.ops:
            for d in op.deps:
                if d.stream == "pe" and op.stream == "pe":
                    continue
                d.inc = True
        counts = {s: 0 for s in STREAMS}
        dma_idx = {s: 0 for s in DMA_STREAMS}
        for op in self.ops:
            if op.stream in DMA_STREAMS:
                i = dma_idx[op.stream]
                dma_idx[op.stream] = i + 1
                op.seq = (i % Prog.NSUB, 16 * (i // Prog.NSUB + 1))
            elif op.inc:
                counts[op.stream] += 1
                op.seq = counts[op.stream]
        per_engine = {e: [] for e in ENGINES}
        for op in self.ops:
            per_engine[STREAM_ENGINE[op.stream]].append(op)
        wm = {}

        def _wait(eng, ename, stream, sub, val):
            key = (ename, stream, sub)
            if val > wm.get(key, 0):
                sem = sems[stream][sub] if sub is not None else sems[stream]
                eng.wait_ge(sem, val)
                wm[key] = val

        def section(eng, ename):
            for op in per_engine[ename]:
                for d in op.deps:
                    if d.seq is None:
                        continue
                    if d.stream in DMA_STREAMS:
                        _wait(eng, ename, d.stream, d.seq[0], d.seq[1])
                    elif not (d.stream == "pe" and op.stream == "pe"):
                        _wait(eng, ename, d.stream, None, d.seq)
                if op.stream in DMA_STREAMS:
                    sub, val = op.seq
                    _wait(eng, ename, op.stream, sub, val - 16)
                inst = op.emit(eng)
                if op.stream in DMA_STREAMS:
                    sub, val = op.seq
                    if inst is None:
                        inst = eng.nop()
                    inst.then_inc(sems[op.stream][sub], 16)
                elif op.inc:
                    if inst is None:
                        inst = eng.nop()
                    inst.then_inc(sems[op.stream], 1)
            if ename == "sync":
                for s in DMA_STREAMS:
                    n = dma_idx[s]
                    for sub in range(Prog.NSUB):
                        k = (n - sub + Prog.NSUB - 1) // Prog.NSUB
                        if k > 0:
                            _wait(eng, ename, s, sub, 16 * k)
                for s in STREAMS:
                    if s not in DMA_STREAMS and counts[s] > 0:
                        _wait(eng, ename, s, None, counts[s])
        return section




def build_program(cfg):
    D, H, T, TL = cfg["D"], cfg["H"], cfg["T"], cfg["TL"]
    KD, KH, NLT, NSB = cfg["KD"], cfg["KH"], cfg["NLT"], cfg["NQT"]
    NHA, HDA, NHB = cfg["NHA"], cfg["HDA"], cfg["NHB"]
    KA = HDA // 128
    NM = NLT // 2
    TH = TL // 2
    KT = TH // 128
    E256, EB = cfg["E256"], cfg["EB"]
    owner, lidx = cfg["owner"], cfg["lidx"]

    nc = bass.Bass("TRN2", target_bir_lowering=False,
                   detect_race_conditions=False)

    def din(name, shape, dt):
        return nc.dram_tensor(name, shape, dt, kind="ExternalInput").ap()

    x_my = din("x_my", [TL, D], F32)
    x_full = din("x_full", [T, D], F32)
    qgT = din("qgT", [H, D], I8)
    quT = din("quT", [H, D], I8)
    qdT = din("qdT", [D, H], I8)
    zgT = din("zgT", [H, KD], I32)
    sgT = din("sgT", [H, KD], F32)
    zuT = din("zuT", [H, KD], I32)
    suT = din("suT", [H, KD], F32)
    zdT = din("zdT", [D, KH], I32)
    sdT = din("sdT", [D, KH], F32)
    bvq = din("bvq", [NHA, HDA], F32)
    bvk = din("bvk", [NHA, HDA], F32)
    bvv = din("bvv", [NHA, HDA], F32)
    wq = din("wq", [D, D], F32)
    wk = din("wk", [D, 128], F32)
    wv = din("wv", [D, 128], F32)
    wo = din("wo", [D, D], F32)
    runtag = din("runtag", [2, 4], I32)
    mska = din("mska", [max(NM, 1), NSB, 128, 256], BF16)
    mskb = din("mskb", [NLT, 128, 512], BF16)
    identity = din("identity", [128, 128], BF16)

    out = nc.dram_tensor("out", [TL, D], F32, kind="ExternalOutput").ap()
    xdbg = nc.dram_tensor("xdbg", [TL, D], F32, kind="ExternalOutput").ap() \
        if DEBUG_XRES else None

    wg_d = nc.dram_tensor("wg_d", [H, D], BF16, kind="Internal").ap()
    wu_d = nc.dram_tensor("wu_d", [H, D], BF16, kind="Internal").ap()
    wd_d = nc.dram_tensor("wd_d", [D, H], BF16, kind="Internal").ap()
    ln1_d = nc.dram_tensor("ln1_d", [T, D], BF16, kind="Internal").ap()
    lnq_d = nc.dram_tensor("lnq_d", [TL, D], BF16, kind="Internal").ap()
    bind_d = nc.dram_tensor("bind_d", [2, NHA, HDA], F32, kind="Internal").ap()
    x_res = nc.dram_tensor("x_res", [TL, D], F32, kind="Internal").ap()
    kv_sh = nc.dram_tensor("kv_sh", [4, TL, 128], BF16, kind="Internal",
                           addr_space="Shared").ap()
    fl_sh = nc.dram_tensor("fl_sh", [2, 8], I32, kind="Internal",
                           addr_space="Shared").ap()

    p = Prog()
    ARC = 96256

    import contextlib
    with contextlib.ExitStack() as _ctx:
        def _cm(x):
            return _ctx.enter_context(x)
        AR = _cm(nc.sbuf_tensor("AR", [128, ARC], BF16))
        stats = _cm(nc.sbuf_tensor("stats", [128, 64], F32))
        zsb = _cm(nc.sbuf_tensor("zsb", [128, 3 * max(KH, KD)], F32))
        zsb_i = _cm(nc.sbuf_tensor("zsb_i", [128, max(KH, KD)], I32))
        qkbT = _cm(nc.sbuf_tensor("qkbT", [128, KD], F32))
        vbB = _cm(nc.sbuf_tensor("vbB", [128, HDA], BF16))
        ident = _cm(nc.sbuf_tensor("ident", [128, 128], BF16))
        identf = _cm(nc.sbuf_tensor("identf", [128, 128], F32))
        ones1 = _cm(nc.sbuf_tensor("ones1", [1, 128], BF16))
        bvb = _cm(nc.sbuf_tensor("bvb", [NHA, 2 * HDA], F32))
        bvrow = _cm(nc.sbuf_tensor("bvrow", [1, HDA], F32))
        bvrow2 = _cm(nc.sbuf_tensor("bvrow2", [1, HDA], BF16))
        flsb = _cm(nc.sbuf_tensor("flsb", [1, 8], I32))
        flone = _cm(nc.sbuf_tensor("flone", [2, 4], I32))
        flzero = _cm(nc.sbuf_tensor("flzero", [2, 4], I32))
        csts = _cm(nc.sbuf_tensor("csts", [128, 8], F32))
        PSA = _cm(nc.psum_tensor("PSA", [128, 2048], F32))
        PSB = _cm(nc.psum_tensor("PSB", [128, 2048], F32))
        s_pe = _cm(nc.semaphore("s_pe"))
        s_dve = _cm(nc.semaphore("s_dve"))
        s_act = _cm(nc.semaphore("s_act"))
        s_pool = _cm(nc.semaphore("s_pool"))
        s_qs = [_cm(nc.semaphore(f"s_qs{i}")) for i in range(Prog.NSUB)]
        s_qa = [_cm(nc.semaphore(f"s_qa{i}")) for i in range(Prog.NSUB)]
        s_qp = [_cm(nc.semaphore(f"s_qp{i}")) for i in range(Prog.NSUB)]
        s_lad = _cm(nc.semaphore("s_lad"))
        block = _cm(nc.Block())
        sems = {"pe": s_pe, "dve": s_dve, "act": s_act, "pool": s_pool,
                "qs": s_qs, "qa": s_qa, "qp": s_qp}

        def R(lo, cols):
            assert 0 <= lo and lo + cols <= ARC, (lo, cols)
            return p.region("AR", lo, lo + cols)

        def A(lo, cols):
            return AR[:, lo:lo + cols]

        def A8(lo, cols_i8):
            return AR[:, lo:lo + (cols_i8 + 1) // 2].bitcast(I8)[:, 0:cols_i8]

        def A32(lo, cols_f32):
            return AR[:, lo:lo + 2 * cols_f32].bitcast(F32)

        def PR(psn, bank, banks=1):
            return p.region(psn, bank * 512, (bank + banks) * 512)

        def dma(stream, out_ap, in_ap, reads=(), writes=(), deps=(), tag=""):
            return p.add(stream,
                         lambda e, o=out_ap, i=in_ap: e.dma_start(out=o, in_=i),
                         reads=reads, writes=writes, deps=deps, tag=tag)

        def dmaT(stream, out_ap, in_ap, reads=(), writes=(), deps=(), tag=""):
            return p.add(stream,
                         lambda e, o=out_ap, i=in_ap: e.dma_start_transpose(o, i),
                         reads=reads, writes=writes, deps=deps, tag=tag)

        r_ident = p.region("ident")
        r_identf = p.region("identf")
        r_ones = p.region("ones1")
        r_flone = p.region("flone")
        r_flzero = p.region("flzero")
        ld_id = dma("qs", ident[:, :], identity, [], [r_ident], tag="ident")
        p.add("dve", lambda e: e.tensor_copy(identf[:, :], ident[:, :]),
              reads=[r_ident], writes=[r_identf], deps=[ld_id])
        p.add("dve", lambda e: e.memset(ones1[:, :], 1.0), writes=[r_ones])
        dma("qp", flone[:, :], runtag, [], [r_flone], tag="tagld")
        p.add("pool", lambda e: e.memset(flzero[:, :], 0), writes=[r_flzero])
        r_csts = p.region("csts")
        p.add("dve", lambda e: e.memset(csts[:, 0:1], 1e-5), writes=[r_csts])

        r_flW = p.region("fl_shW")

        def _fl_zero(e):
            pid = e.partition_id()
            par = e.snap(pid % 2, min_val=0, max_val=1)
            return e.dma_start(out=fl_sh[:, bass.ds(par * 4, 4)],
                               in_=flzero[0:2, 0:4])
        p.add("qp", _fl_zero, reads=[r_flzero], writes=[r_flW], tag="flz")

        PIECE = 4096
        QI0, QI1 = 92160, 94208
        WQ0, WQ1 = 83968, 88064
        r_qi = [R(QI0, 2048), R(QI1, 2048)]
        qi_ap = [A8(QI0, PIECE), A8(QI1, PIECE)]
        r_wdq = [R(WQ0, 4096), R(WQ1, 4096)]
        wdq_ap = [A(WQ0, 4096), A(WQ1, 4096)]
        r_zs = p.region("zsb")
        r_zsi = p.region("zsb_i")
        wdeq_regions = {}

        def emit_dequant(tag, qsrc, zsrc, ssrc, wdst, n_out, n_in):
            ngr = n_in // 128
            npc = (n_in + PIECE - 1) // PIECE
            for ot in range(n_out):
                ld_z = dma("qp", zsb_i[:, 0:ngr], zsrc[ot * 128:(ot + 1) * 128, :],
                           [], [r_zsi], tag=f"z{tag}{ot}")
                cvt = p.add("dve",
                            lambda e, g=ngr: e.tensor_copy(zsb[:, 0:g],
                                                           zsb_i[:, 0:g]),
                            reads=[r_zsi], writes=[r_zs], deps=[ld_z])
                ld_s = dma("qp", zsb[:, ngr:2 * ngr],
                           ssrc[ot * 128:(ot + 1) * 128, :],
                           [], [r_zs], deps=[cvt], tag=f"s{tag}{ot}")
                nzs = p.add("dve", lambda e, g=ngr: e.scalar_tensor_tensor(
                    out=zsb[:, 2 * g:3 * g], in0=zsb[:, 0:g], scalar=-1.0,
                    in1=zsb[:, g:2 * g], op0=ALU.mult, op1=ALU.mult),
                    reads=[r_zs], writes=[r_zs], deps=[ld_s, cvt])
                for pc in range(npc):
                    w = min(PIECE, n_in - pc * PIECE)
                    sl = (ot * npc + pc) % 2
                    ld_q = dma("qp", qi_ap[sl][:, 0:w],
                               qsrc[ot * 128:(ot + 1) * 128,
                                    pc * PIECE:pc * PIECE + w],
                               [], [r_qi[sl]], tag=f"q{tag}{ot}.{pc}")
                    last = None
                    use_act = (ot * npc + pc) % 3 == 2
                    for c in range(w // 128):
                        g = pc * (PIECE // 128) + c
                        if use_act:
                            last = p.add(
                                "act",
                                lambda e, sl=sl, c=c, g=g: e.activation(
                                    out=wdq_ap[sl][:, c * 128:(c + 1) * 128],
                                    in_=qi_ap[sl][:, c * 128:(c + 1) * 128],
                                    func=AF.Identity,
                                    scale=zsb[:, ngr + g:ngr + g + 1],
                                    bias=zsb[:, 2 * ngr + g:2 * ngr + g + 1]),
                                reads=[r_qi[sl], r_zs], writes=[r_wdq[sl]],
                                deps=[ld_q, nzs], tag=f"dq{tag}{ot}.{pc}.{c}")
                        else:
                            last = p.add(
                                "dve",
                                lambda e, sl=sl, c=c, g=g: e.tensor_scalar(
                                    out=wdq_ap[sl][:, c * 128:(c + 1) * 128],
                                    in0=qi_ap[sl][:, c * 128:(c + 1) * 128],
                                    scalar1=zsb[:, g:g + 1],
                                    scalar2=zsb[:, ngr + g:ngr + g + 1],
                                    op0=ALU.subtract, op1=ALU.mult),
                                reads=[r_qi[sl], r_zs], writes=[r_wdq[sl]],
                                deps=[ld_q, ld_s], tag=f"dq{tag}{ot}.{pc}.{c}")
                    rr = p.region(f"w{tag}", ot * npc + pc, ot * npc + pc + 1)
                    dma("qa" if (ot + pc) % 2 else "qs",
                        wdst[ot * 128:(ot + 1) * 128, pc * PIECE:pc * PIECE + w],
                        wdq_ap[sl][:, 0:w], [r_wdq[sl]], [rr], deps=[last],
                        tag=f"wst{tag}{ot}.{pc}")
                    wdeq_regions[(tag, ot, pc)] = rr

        deq_tasks = []
        if 0 in PHASES:
            deq_tasks = [
                lambda: emit_dequant("g", qgT, zgT, sgT, wg_d, KH, D),
                lambda: emit_dequant("u", quT, zuT, suT, wu_d, KH, D),
                lambda: emit_dequant("d", qdT, zdT, sdT, wd_d, KD, H),
            ]

        def wsrc_regions(tag, row_lo, row_hi, col_lo, col_hi, n_in):
            npc = (n_in + PIECE - 1) // PIECE
            rs = []
            for ot in range(row_lo // 128, (row_hi + 127) // 128):
                for pc in range(col_lo // PIECE,
                                min(npc, (col_hi + PIECE - 1) // PIECE)):
                    rs.append(wdeq_regions[(tag, ot, pc)])
            return rs

        def emit_ln(x_src_ap, x_reads, xb_lo, junk_lo, dst_lo, sidx, tag):
            JW = min(1024, D)
            npc = D // JW
            xb = A32(xb_lo, D)
            r_xb = R(xb_lo, 2 * D)
            r_junk = R(junk_lo, 2 * JW)
            r_st = p.region("stats", sidx * 16, sidx * 16 + 16)
            st = stats[:, sidx * 16:sidx * 16 + 16]
            r_dst = R(dst_lo, D)
            ld = dma("qs", xb[:, :], x_src_ap, list(x_reads), [r_xb], tag=tag)
            s1 = p.add("dve", lambda e: e.reduce_sum(st[:, 0:1], xb[:, :],
                                                     axis=AX.X),
                       reads=[r_xb], writes=[r_st], deps=[ld])
            p.add("dve", lambda e: e.tensor_scalar_mul(st[:, 1:2], st[:, 0:1],
                                                       -1.0 / D),
                  reads=[r_st], writes=[r_st], deps=[s1])
            sq = []
            for hf in range(npc):
                q1 = p.add(
                    "act", lambda e, hf=hf: e.activation(
                        out=A32(junk_lo, JW),
                        in_=xb[:, hf * JW:(hf + 1) * JW],
                        func=AF.Square, bias=st[:, 1:2], scale=1.0),
                    reads=[r_xb, r_st], writes=[r_junk], deps=[s1])
                sq.append(p.add(
                    "dve", lambda e, hf=hf: e.reduce_sum(
                        st[:, 2 + hf:3 + hf], A32(junk_lo, JW), axis=AX.X),
                    reads=[r_junk], writes=[r_st], deps=[q1]))
            acc = sq[0]
            for hf in range(1, npc):
                acc = p.add("dve", lambda e, hf=hf: e.tensor_add(
                    st[:, 2:3], st[:, 2:3], st[:, 2 + hf:3 + hf]),
                    reads=[r_st], writes=[r_st], deps=[acc, sq[hf]])
            s4 = p.add("act", lambda e: e.activation(out=st[:, 6:7],
                                                     in_=st[:, 2:3],
                                                     func=AF.Sqrt,
                                                     bias=csts[:, 0:1],
                                                     scale=1.0 / D),
                       reads=[r_st, r_csts], writes=[r_st], deps=[acc])
            s5 = p.add("dve", lambda e: e.reciprocal(st[:, 7:8], st[:, 6:7]),
                       reads=[r_st], writes=[r_st], deps=[s4])
            ln = p.add("dve", lambda e: e.tensor_scalar(
                out=A(dst_lo, D), in0=xb[:, :], scalar1=st[:, 1:2],
                scalar2=st[:, 7:8], op0=ALU.add, op1=ALU.mult),
                reads=[r_xb, r_st], writes=[r_dst], deps=[s5], tag=tag + "f")
            return ln, r_dst

        trsl = PSB[:, 1536:2048].bitcast(BF16)
        r_trsl = PR("PSB", 3)

        def emit_tr(src_lo, dst_lo, deps=(), scalar1=None, evac="dve",
                    slot=0, tag="tr"):
            r_src = R(src_lo, 128)
            r_dst = R(dst_lo, 128)
            mm = p.add("pe", lambda e: e.transpose(
                trsl[:, slot * 128:(slot + 1) * 128], A(src_lo, 128),
                ident[:, :]),
                reads=[r_src, r_ident], writes=[r_trsl], deps=list(deps),
                tag=tag)
            if scalar1 is None:
                if evac == "act":
                    fn = lambda e: e.copy(A(dst_lo, 128),
                                          trsl[:, slot * 128:(slot + 1) * 128])
                else:
                    fn = lambda e: e.tensor_copy(
                        A(dst_lo, 128), trsl[:, slot * 128:(slot + 1) * 128])
                ev = p.add(evac, fn, reads=[r_trsl], writes=[r_dst], deps=[mm],
                           tag=tag + "e")
            else:
                if evac == "act":
                    fn = lambda e: e.mul(A(dst_lo, 128),
                                         trsl[:, slot * 128:(slot + 1) * 128],
                                         scalar1)
                else:
                    fn = lambda e: e.tensor_scalar(
                        out=A(dst_lo, 128),
                        in0=trsl[:, slot * 128:(slot + 1) * 128],
                        scalar1=scalar1, scalar2=None, op0=ALU.mult)
                ev = p.add(evac, fn, reads=[r_trsl], writes=[r_dst], deps=[mm],
                           tag=tag + "e")
            return ev

        r_bvb = p.region("bvb")
        r_bvrow = p.region("bvrow")
        r_qkbT = p.region("qkbT")
        r_vbB = p.region("vbB")
        r_bind = p.region("bind_d")

        ldb = [dma("qs", bvb[:, i * HDA:(i + 1) * HDA], src, [], [r_bvb],
                   tag=f"bv{i}")
               for i, src in enumerate((bvq, bvk))]

        BTMP = 80640

        def bind_seq(col, sidx):
            src = bvb[:, col:col + HDA]
            r_c = p.region("bvb", col, col + HDA)
            st = stats[:, 16 + sidx * 4:20 + sidx * 4]
            r_st = p.region("stats", 16 + sidx * 4, 20 + sidx * 4)
            r_tmp = R(BTMP, HDA)
            a1 = p.add("dve", lambda e: e.reduce_sum(
                st[0:NHA, 0:1], src, axis=AX.X, apply_absolute_value=True),
                reads=[r_c], writes=[r_st], deps=list(ldb))
            a2 = p.add("dve", lambda e: e.tensor_scalar_mul(
                st[0:NHA, 1:2], st[0:NHA, 0:1], 1.0 / HDA),
                reads=[r_st], writes=[r_st], deps=[a1])
            sg = p.add("act", lambda e: e.activation(out=src, in_=src,
                                                     func=AF.Sign),
                       reads=[r_c], writes=[r_c], deps=[a1])
            t1 = p.add("act", lambda e: e.activation(
                out=A(BTMP, HDA)[0:NHA, :], in_=src, func=AF.Square),
                reads=[r_c], writes=[r_tmp], deps=[sg])
            t2 = p.add("dve", lambda e: e.tensor_sub(
                src, src, A(BTMP, HDA)[0:NHA, :]),
                reads=[r_c, r_tmp], writes=[r_c], deps=[t1])
            t3 = p.add("dve", lambda e: e.tensor_scalar_add(src, src, 1.0),
                       reads=[r_c], writes=[r_c], deps=[t2])
            fin = p.add("dve", lambda e: e.tensor_scalar(
                out=src, in0=src, scalar1=st[0:NHA, 1:2], scalar2=None,
                op0=ALU.mult),
                reads=[r_c, r_st], writes=[r_c], deps=[t3, a2])
            return fin

        bq = bind_seq(0, 0)
        bk = bind_seq(HDA, 1)
        qkb = p.add("dve", lambda e: e.tensor_mul(
            bvb[:, 0:HDA], bvb[:, 0:HDA], bvb[:, HDA:2 * HDA]),
            reads=[r_bvb], writes=[p.region("bvb", 0, HDA)], deps=[bq, bk])
        st_qkb = dma("qs", bind_d[0], bvb[0:NHA, 0:HDA], [r_bvb], [r_bind],
                     deps=[qkb], tag="qkbst")
        ldb.append(dma("qs", bvb[:, HDA:2 * HDA], bvv, [r_bvb],
                       [p.region("bvb", HDA, 2 * HDA)], deps=[qkb],
                       tag="bv2"))
        bv_ = bind_seq(HDA, 2)
        st_vb = dma("qs", bind_d[1], bvb[0:NHA, HDA:2 * HDA], [r_bvb],
                    [r_bind], deps=[bv_], tag="vbst")

        QKF = 80128
        r_qkf = R(QKF, 256)
        z0 = p.add("dve", lambda e: e.memset(A32(QKF, 128), 0.0),
                   writes=[r_qkf])
        ld_qk = dma("qs", A32(QKF, 128)[0:KD, 0:128],
                    bind_d[0].rearrange("h a -> (h a)").rearrange(
                        "(c p) -> c p", p=128),
                    [r_bind], [r_qkf], deps=[st_qkb, z0], tag="qkbld")
        ps_qk = PSB[:, 1024:1536]
        r_psqk = PR("PSB", 2)
        tr_qk = p.add("pe", lambda e: e.transpose(
            ps_qk[:, 0:128], A32(QKF, 128), identf[:, :]),
            reads=[r_qkf, r_identf], writes=[r_psqk], deps=[ld_qk])
        ev_qk = p.add("dve", lambda e: e.tensor_copy(qkbT[:, :],
                                                     ps_qk[:, 0:KD]),
                      reads=[r_psqk], writes=[r_qkbT], deps=[tr_qk])

        XB0, XB1, JUNK, LNS = 51456, 59648, 67840, 76032
        r_ln1 = [p.region("ln1_d", g, g + 1) for g in range(NSB)]
        r_lnq = [p.region("lnq_d", k, k + 1) for k in range(NLT)]
        for i in range(NSB + NLT):
            if i < NSB:
                src = x_full[i * 128:(i + 1) * 128, :]
                x_reads = []
            else:
                k = i - NSB
                src = x_my[k * 128:(k + 1) * 128, :]
                x_reads = []
            ln_op, r_lt = emit_ln(src, x_reads, XB0 if i % 2 == 0 else XB1,
                                  JUNK, LNS, 2 + i % 2, tag=f"lnA{i}")
            if i < NSB:
                dma("qa", ln1_d[i * 128:(i + 1) * 128, :], A(LNS, D),
                    [r_lt], [r_ln1[i]], deps=[ln_op], tag=f"l1s{i}")
            else:
                dma("qa", lnq_d[(i - NSB) * 128:(i - NSB + 1) * 128, :],
                    A(LNS, D), [r_lt], [r_lnq[i - NSB]], deps=[ln_op],
                    tag=f"lqs{i}")

        LHB, KSB, QTH, PTA = 0, 16384, 32768, 40960
        YHT, YBLK, LQB, MSK, SIGT = 45056, 47104, 49152, 50176, 50688

        def xres_reg(q, c0, c1):
            return p.region("x_res", q * D + c0, q * D + c1)

        def emit_head(h):
            ldv = dma("qs", bvrow[0:1, 0:HDA], bind_d[1][h], [r_bind],
                      [r_bvrow], deps=[st_vb], tag=f"vbh{h}")
            cvv = p.add("dve", lambda e: e.tensor_copy(
                bvrow2[0:1, 0:HDA], bvrow[0:1, 0:HDA]),
                reads=[r_bvrow], writes=[p.region("bvrow2")], deps=[ldv])
            vb_evs = []
            VW = min(512, HDA)
            for hf in range((HDA + VW - 1) // VW):
                mm = p.add("pe", lambda e, hf=hf: e.matmul(
                    PSB[:, 0:VW], ones1[:, :],
                    bvrow2[0:1, hf * VW:(hf + 1) * VW],
                    start=True, stop=True),
                    reads=[p.region("bvrow2"), r_ones], writes=[PR("PSB", 0)],
                    deps=[cvv])
                vb_evs.append(p.add("act", lambda e, hf=hf: e.copy(
                    vbB[:, hf * VW:(hf + 1) * VW], PSB[:, 0:VW]),
                    reads=[PR("PSB", 0)], writes=[r_vbB], deps=[mm]))

            r_lhb = [R(LHB + g * HDA, HDA) for g in range(NSB)]
            lhb_ld = [dma("qs", A(LHB + g * HDA, HDA),
                          ln1_d[g * 128:(g + 1) * 128, h * HDA:(h + 1) * HDA],
                          [r_ln1[g]], [r_lhb[g]], tag=f"lh{h}.{g}")
                      for g in range(NSB)]
            k_ev = {}
            for g in range(NSB):
                for dc in range(KA):
                    k_ev[(g, dc)] = emit_tr(
                        LHB + g * HDA + dc * 128, KSB + dc * (NSB * 128) + g * 128,
                        deps=[lhb_ld[g]], slot=(g * KA + dc) % 4,
                        evac="act" if (g + dc) % 2 else "dve",
                        tag=f"K{h}.{g}.{dc}")
            v_op = {}
            for g in range(NSB):
                deps = [k_ev[(g, dc)] for dc in range(KA)] + vb_evs
                v_op[g] = p.add("dve", lambda e, g=g: e.tensor_mul(
                    A(LHB + g * HDA, HDA), A(LHB + g * HDA, HDA), vbB[:, 0:HDA]),
                    reads=[r_lhb[g], r_vbB], writes=[r_lhb[g]], deps=deps,
                    tag=f"V{h}.{g}")

            q_ev = {}
            for k in range(NLT):
                ldq2 = dma("qa", A(LQB, HDA),
                           lnq_d[k * 128:(k + 1) * 128, h * HDA:(h + 1) * HDA],
                           [r_lnq[k]], [R(LQB, HDA)], tag=f"lq{h}.{k}")
                for dc in range(KA):
                    gc = h * KA + dc
                    q_ev[(k, dc)] = emit_tr(
                        LQB + dc * 128, QTH + dc * TL + k * 128,
                        deps=[ldq2, ev_qk], scalar1=qkbT[:, gc:gc + 1],
                        slot=(k + dc) % 4, tag=f"Q{h}.{k}.{dc}")

            for m in range(NM):
                E = E256[m]
                pt_op = {}
                for j in range(E):
                    bank = j % 2
                    ps = PSA[:, bank * 512:bank * 512 + 256]
                    r_ps = PR("PSA", bank)
                    mm = None
                    for dc in range(KA):
                        mm = p.add("pe", lambda e, ps=ps, j=j, dc=dc, m=m:
                                   e.matmul(ps,
                                            A(KSB + dc * NSB * 128 + j * 128, 128),
                                            A(QTH + dc * TL + m * 256, 256),
                                            start=(dc == 0), stop=(dc == KA - 1)),
                                   reads=[R(KSB + dc * NSB * 128 + j * 128, 128),
                                          R(QTH + dc * TL + m * 256, 256)],
                                   writes=[r_ps],
                                   deps=[k_ev[(j, dc)], q_ev[(2 * m, dc)],
                                         q_ev[(2 * m + 1, dc)]],
                                   tag=f"sa{h}.{m}.{j}")
                    sg2 = p.add("act", lambda e, ps=ps: e.activation(
                        out=A(SIGT, 256), in_=ps, func=AF.Sigmoid, scale=0.125),
                        reads=[r_ps], writes=[R(SIGT, 256)], deps=[mm],
                        tag=f"sg{h}.{m}.{j}")
                    msl = MSK + (j % 2) * 256
                    mld = dma("qa", A(msl, 256), mska[m, j], [],
                              [R(msl, 256)], tag=f"ma{m}.{j}")
                    pt_op[j] = p.add("dve", lambda e, j=j, msl=msl: e.tensor_mul(
                        A(PTA + j * 256, 256), A(SIGT, 256), A(msl, 256)),
                        reads=[R(SIGT, 256), R(msl, 256)],
                        writes=[R(PTA + j * 256, 256)], deps=[sg2, mld],
                        tag=f"pt{h}.{m}.{j}")
                ytr = {}
                for dc in range(KA):
                    ps = PSB[:, 512:768]
                    r_ps = PR("PSB", 1)
                    mm = None
                    for j in range(E):
                        mm = p.add("pe", lambda e, ps=ps, j=j, dc=dc, E=E: e.matmul(
                            ps, A(LHB + j * HDA + dc * 128, 128),
                            A(PTA + j * 256, 256),
                            start=(j == 0), stop=(j == E - 1)),
                            reads=[r_lhb[j], R(PTA + j * 256, 256)],
                            writes=[r_ps], deps=[v_op[j], pt_op[j]],
                            tag=f"oa{h}.{m}.{dc}")
                    ev = p.add("act", lambda e, ps=ps, dc=dc: e.copy(
                        A(YHT + dc * 256, 256), ps),
                        reads=[r_ps], writes=[R(YHT + dc * 256, 256)],
                        deps=[mm])
                    for ts in range(2):
                        ytr[(dc, ts)] = emit_tr(
                            YHT + dc * 256 + ts * 128,
                            YBLK + ts * 1024 + dc * 128,
                            deps=[ev], slot=(dc + ts) % 4,
                            tag=f"yb{h}.{m}.{dc}.{ts}")
                for ts in range(2):
                    q = 2 * m + ts
                    deps = [ytr[(dc, ts)] for dc in range(KA)]
                    xb2 = A32(XB0, 2048)
                    r_xb2 = R(XB0, 4096)
                    ld = dma("qs", xb2[:, 0:HDA],
                             x_my[q * 128:(q + 1) * 128, h * HDA:(h + 1) * HDA],
                             [], [r_xb2], deps=deps, tag=f"xa{h}.{q}")
                    ad = p.add("dve", lambda e, ts=ts: e.tensor_add(
                        xb2[:, 0:HDA], xb2[:, 0:HDA], A(YBLK + ts * 1024, HDA)),
                        reads=[r_xb2, R(YBLK + ts * 1024, HDA)],
                        writes=[r_xb2], deps=[ld] + deps)
                    dma("qs", x_res[q * 128:(q + 1) * 128,
                                    h * HDA:(h + 1) * HDA],
                        xb2[:, 0:HDA], [r_xb2],
                        [xres_reg(q, h * HDA, (h + 1) * HDA)], deps=[ad],
                        tag=f"xs{h}.{q}")

        for h in range(NHA):
            if h < len(deq_tasks):
                deq_tasks[h]()
            emit_head(h)
        for i in range(NHA, len(deq_tasks)):
            deq_tasks[i]()

        def emit_ffn(x_src, x_dst, dst_name, phase):
            LNT, ACTT = 0, 16384
            SG, ATOK = 60416, 64512
            WM0 = 68608
            XS = 71680
            LXB, LJUNK = 73728, 81920
            LLN = 60416
            HBW = 1024
            nhb = (H + HBW - 1) // HBW
            for th in range(2):
                for kt in range(KT):
                    q = th * KT + kt
                    ln_op, r_lt = emit_ln(
                        x_src[q * 128:(q + 1) * 128, :],
                        [p.region("x_res", q * D, (q + 1) * D)],
                        LXB, LJUNK, LLN, 2 + kt % 2, tag=f"lnF{phase}.{q}")
                    for c in range(KD):
                        emit_tr(LLN + c * 128, LNT + c * TH + kt * 128,
                                deps=[ln_op], slot=c % 4,
                                evac="act" if c % 2 else "dve",
                                tag=f"lt{phase}.{q}.{c}")
                for hb in range(nhb):
                    w = min(HBW, H - hb * HBW)
                    nsub = (w + 511) // 512
                    mm_last = {}
                    for tag_m, wsrc in (("g", wg_d), ("u", wu_d)):
                        for c in range(KD):
                            wml = WM0 + (c % 3) * HBW
                            r_wm = R(wml, HBW)
                            ldw = dmaT("qs", A(wml, w),
                                       wsrc[hb * HBW:hb * HBW + w,
                                            c * 128:(c + 1) * 128],
                                       wsrc_regions(tag_m, hb * HBW,
                                                    hb * HBW + w, c * 128,
                                                    (c + 1) * 128, D),
                                       [r_wm],
                                       tag=f"W{tag_m}{phase}.{th}.{hb}.{c}")
                            for sub in range(nsub):
                                sw = min(512, w - sub * 512)
                                ps_ap = PSA if sub == 0 else PSB
                                psn = "PSA" if sub == 0 else "PSB"
                                for kt in range(KT):
                                    ps = ps_ap[:, kt * 512:kt * 512 + sw]
                                    mm_last[(tag_m, sub, kt)] = p.add(
                                        "pe",
                                        lambda e, ps=ps, c=c, kt=kt, wml=wml,
                                        sub=sub, sw=sw:
                                        e.matmul(ps,
                                                 A(LNT + c * TH + kt * 128, 128),
                                                 A(wml + sub * 512, sw),
                                                 start=(c == 0),
                                                 stop=(c == KD - 1)),
                                        reads=[R(LNT + c * TH + kt * 128, 128),
                                               r_wm],
                                        writes=[PR(psn, kt)], deps=[ldw],
                                        tag=f"M{tag_m}{phase}.{th}.{hb}.{c}.{sub}.{kt}")
                        if tag_m == "g":
                            for sub in range(nsub):
                                sw = min(512, w - sub * 512)
                                ps_ap = PSA if sub == 0 else PSB
                                psn = "PSA" if sub == 0 else "PSB"
                                for kt in range(KT):
                                    mm_last[("sg", sub, kt)] = p.add(
                                        "act",
                                        lambda e, kt=kt, sub=sub, sw=sw,
                                        ps_ap=ps_ap: e.activation(
                                            out=A(SG + kt * HBW + sub * 512, sw),
                                            in_=ps_ap[:, kt * 512:kt * 512 + sw],
                                            func=AF.Silu),
                                        reads=[PR(psn, kt)],
                                        writes=[R(SG + kt * HBW + sub * 512, sw)],
                                        deps=[mm_last[("g", sub, kt)]],
                                        tag=f"si{phase}.{th}.{hb}.{sub}.{kt}")
                    at_ops = {}
                    for sub in range(nsub):
                        sw = min(512, w - sub * 512)
                        ps_ap = PSA if sub == 0 else PSB
                        psn = "PSA" if sub == 0 else "PSB"
                        for kt in range(KT):
                            at_ops[(sub, kt)] = p.add(
                                "dve", lambda e, kt=kt, sub=sub, sw=sw,
                                ps_ap=ps_ap: e.tensor_mul(
                                    A(ATOK + kt * HBW + sub * 512, sw),
                                    A(SG + kt * HBW + sub * 512, sw),
                                    ps_ap[:, kt * 512:kt * 512 + sw]),
                                reads=[R(SG + kt * HBW + sub * 512, sw),
                                       PR(psn, kt)],
                                writes=[R(ATOK + kt * HBW + sub * 512, sw)],
                                deps=[mm_last[("sg", sub, kt)],
                                      mm_last[("u", sub, kt)]],
                                tag=f"am{phase}.{th}.{hb}.{sub}.{kt}")
                    for kt in range(KT):
                        for hc in range(w // 128):
                            ch = hb * (HBW // 128) + hc
                            emit_tr(ATOK + kt * HBW + hc * 128,
                                    ACTT + ch * TH + kt * 128,
                                    deps=[at_ops[(hc // 4, kt)]],
                                    slot=(kt + hc) % 4,
                                    evac="act" if hc % 2 else "dve",
                                    tag=f"at{phase}.{th}.{hb}.{kt}.{hc}")
                DBW = min(1024, D)
                for db in range(D // DBW):
                    ndsub = DBW // 512
                    mm_last = {}
                    for c in range(KH):
                        wml = WM0 + (c % 3) * HBW
                        r_wm = R(wml, HBW)
                        ldw = dmaT("qs", A(wml, DBW),
                                   wd_d[db * DBW:(db + 1) * DBW,
                                        c * 128:(c + 1) * 128],
                                   wsrc_regions("d", db * DBW, (db + 1) * DBW,
                                                c * 128, (c + 1) * 128, H),
                                   [r_wm], tag=f"Wd{phase}.{th}.{db}.{c}")
                        for sub in range(ndsub):
                            ps_ap = PSA if sub == 0 else PSB
                            psn = "PSA" if sub == 0 else "PSB"
                            for kt in range(KT):
                                ps = ps_ap[:, kt * 512:(kt + 1) * 512]
                                mm_last[(sub, kt)] = p.add(
                                    "pe", lambda e, ps=ps, c=c, kt=kt, wml=wml,
                                    sub=sub:
                                    e.matmul(ps, A(ACTT + c * TH + kt * 128, 128),
                                             A(wml + sub * 512, 512),
                                             start=(c == 0),
                                             stop=(c == KH - 1)),
                                    reads=[R(ACTT + c * TH + kt * 128, 128),
                                           r_wm],
                                    writes=[PR(psn, kt)], deps=[ldw],
                                    tag=f"Md{phase}.{th}.{db}.{c}.{sub}.{kt}")
                    for sub in range(ndsub):
                        ps_ap = PSA if sub == 0 else PSB
                        psn = "PSA" if sub == 0 else "PSB"
                        for kt in range(KT):
                            q = th * KT + kt
                            col = db * DBW + sub * 512
                            sl = (kt + sub * KT) % 2
                            xs = A32(XS + sl * 1024, 512)
                            r_xs = R(XS + sl * 1024, 1024)
                            ld = dma("qs", xs[:, 0:512],
                                     x_src[q * 128:(q + 1) * 128, col:col + 512],
                                     [p.region("x_res", q * D + col,
                                               q * D + col + 512)],
                                     [r_xs], tag=f"xf{phase}.{th}.{db}.{sub}.{kt}")
                            ad = p.add("dve", lambda e, kt=kt, ps_ap=ps_ap,
                                       xs=xs: e.tensor_add(
                                           xs[:, 0:512], xs[:, 0:512],
                                           ps_ap[:, kt * 512:(kt + 1) * 512]),
                                       reads=[r_xs, PR(psn, kt)], writes=[r_xs],
                                       deps=[ld, mm_last[(sub, kt)]])
                            dma("qs", x_dst[q * 128:(q + 1) * 128, col:col + 512],
                                xs[:, 0:512], [r_xs],
                                [p.region(dst_name, q * D + col,
                                          q * D + col + 512)],
                                deps=[ad], tag=f"xo{phase}.{th}.{db}.{sub}.{kt}")

        if 2 in PHASES:
            emit_ffn(x_res, x_res, "x_res", 2)

        if 3 in PHASES:
            LNT3, QT3 = 0, 32768
            OUTT = 0
            WQF, WQB = 65536, 73728
            KSB3, VSB3, KVT = 77824, 79872, 81920
            PB3, PTB3, MB3, XS3 = 86016, 88064, 90112, 94208
            L3XB, L3JK, L3LN = 65536, 73728, 81920

            for kt in range(NLT):
                ln_op, _ = emit_ln(x_res[kt * 128:(kt + 1) * 128, :],
                                   [p.region("x_res", kt * D, (kt + 1) * D)],
                                   L3XB, L3JK, L3LN, 2 + kt % 2, tag=f"ln3.{kt}")
                for c in range(KD):
                    emit_tr(L3LN + c * 128, LNT3 + c * TL + kt * 128,
                            deps=[ln_op], slot=c % 4,
                            evac="act" if c % 2 else "dve", tag=f"l3t{kt}.{c}")

            def proj(w_in, col0, dst_lo, tagp):
                ldf = dma("qa", A32(WQF, KD * 128).rearrange("p (c n) -> p c n", c=KD),
                          w_in.rearrange("(c p) n -> p c n", p=128)[:, :,
                                                                   col0:col0 + 128],
                          [], [R(WQF, 2 * KD * 128)], tag=f"pw{tagp}")
                cv = p.add("dve", lambda e: e.tensor_copy(A(WQB, KD * 128),
                                                          A32(WQF, KD * 128)),
                           reads=[R(WQF, 2 * KD * 128)], writes=[R(WQB, KD * 128)],
                           deps=[ldf], tag=f"pc{tagp}")
                evs = []
                PW = min(512, TL)
                for thl in range(TL // PW):
                    mm = None
                    for c in range(KD):
                        mm = p.add("pe", lambda e, c=c, thl=thl: e.matmul(
                            PSB[:, 0:PW], A(WQB + c * 128, 128),
                            A(LNT3 + c * TL + thl * PW, PW),
                            start=(c == 0), stop=(c == KD - 1)),
                            reads=[R(WQB + c * 128, 128),
                                   R(LNT3 + c * TL + thl * PW, PW)],
                            writes=[PR("PSB", 0)], deps=[cv], tag=f"pm{tagp}.{thl}")
                    evs.append(p.add("act", lambda e, thl=thl: e.copy(
                        A(dst_lo + thl * PW, PW), PSB[:, 0:PW]),
                        reads=[PR("PSB", 0)], writes=[R(dst_lo + thl * PW, PW)],
                        deps=[mm], tag=f"pe{tagp}.{thl}"))
                return evs

            k_ev3 = proj(wk, 0, KSB3, "k")
            v_ev3 = proj(wv, 0, VSB3, "v")

            kv_tr = []
            for slot, src_lo in ((0, KSB3), (1, VSB3)):
                for tt in range(NLT):
                    kv_tr.append(emit_tr(src_lo + tt * 128,
                                         KVT + slot * (NLT * 128) + tt * 128,
                                         deps=(k_ev3 if slot == 0 else v_ev3),
                                         slot=tt % 4, tag=f"kv{slot}.{tt}"))

            r_kvW = p.region("kv_shW")
            r_kvt = R(KVT, 2 * NLT * 128)
            ladder_n = cfg["ladder"]

            def _kv_exchange(e):
                pid = e.partition_id()
                par = e.snap(pid % 2, min_val=0, max_val=1)
                kv4 = kv_sh
                for slot in range(2):
                    e.dma_start(
                        out=kv4[e.snap(par * 2 + slot, min_val=0, max_val=3)]
                        .rearrange("(tt pp) d -> pp tt d", pp=128),
                        in_=A(KVT + slot * NLT * 128, NLT * 128)
                        .rearrange("pp (tt d) -> pp tt d", tt=NLT),
                    ).then_inc(s_lad, 16)
                    e.wait_ge(s_lad, 16 * (slot + 1))
                e.dma_start(out=fl_sh[:, bass.ds(par * 4, 4)],
                            in_=flone[0:2, 0:4]).then_inc(s_lad, 16)
                e.wait_ge(s_lad, 48)
                with (
                    e.register("f0") as f0,
                    e.register("f1") as f1,
                    e.register("tg") as tg,
                    e.register("ptgt") as ptgt,
                    e.register("zr") as zr,
                ):
                    e.reg_mov(f0, 0)
                    e.reg_mov(f1, 0)
                    e.reg_load(tg, flone[0:1, 0:1])
                    e.reg_mov(ptgt, 48)
                    e.reg_alu(zr, par, 1, ALU.logical_shift_right)
                    zidx = e.snap(zr, min_val=0, max_val=1)
                    for _ in range(ladder_n):
                        nd = e.compute_val((f0 != tg) | (f1 != tg))
                        e.dma_start(out=flsb[0:1, 0:8], in_=fl_sh[zidx][None, :],
                                    cond=nd).then_inc(s_lad, 16)
                        c16 = e.compute_val(nd * 16)
                        e.reg_alu(ptgt, ptgt, c16, ALU.add)
                        e.wait_ge(s_lad, ptgt)
                        e.reg_load(f0, flsb[0:1, 0:1])
                        e.reg_load(f1, flsb[0:1, 4:5])
                return e.nop()

            lad_done = p.add("pool", _kv_exchange,
                             reads=[r_kvt, r_flone, p.region("flsb")],
                             writes=[r_kvW, r_flW, p.region("flsb")],
                             deps=kv_tr, tag="kvex")

            q_ev3 = {hq: proj(wq, hq * 128, QT3 + hq * TL, f"q{hq}")
                     for hq in range(NHB)}

            vvld, kkld = {}, {}
            for g in range(NSB):
                w_, l_ = owner[g], lidx[g]
                vvld[g] = dma("qp", A(VSB3 + g * 128, 128),
                              kv_sh[2 * w_ + 1, l_ * 128:(l_ + 1) * 128, :],
                              [r_kvW], [R(VSB3 + g * 128, 128)], deps=[lad_done],
                              tag=f"vl{g}")
                ktmp = dma("qp", A(KVT + (g % 2) * 128, 128),
                           kv_sh[2 * w_, l_ * 128:(l_ + 1) * 128, :],
                           [r_kvW], [R(KVT + (g % 2) * 128, 128)],
                           deps=[lad_done], tag=f"kl{g}")
                kkld[g] = emit_tr(KVT + (g % 2) * 128, KSB3 + g * 128,
                                  deps=[ktmp], slot=g % 4, tag=f"kt3{g}")

            mb_ld = [dma("qa", A(MB3 + k * 512, 512), mskb[k], [],
                         [R(MB3 + k * 512, 512)], tag=f"mb{k}")
                     for k in range(NLT)]

            ot_ev = {}
            for h in range(NHB):
                for k in range(NLT):
                    E = EB[k]
                    mm = {}
                    for j in range(E):
                        mm[j] = p.add("pe", lambda e, h=h, k=k, j=j: e.matmul(
                            PSA[:, j * 512:(j + 1) * 512],
                            A(QT3 + h * TL + k * 128, 128),
                            A(KSB3 + j * 512, 512), start=True, stop=True),
                            reads=[R(QT3 + h * TL + k * 128, 128)] +
                            [R(KSB3 + g * 128, 128)
                             for g in range(4 * j, min(4 * j + 4, NSB))],
                            writes=[PR("PSA", j)],
                            deps=[q_ev3[h][(k * 128) // min(512, TL)]] +
                            [kkld[g] for g in range(4 * j, min(4 * j + 4, NSB))],
                            tag=f"sb{h}.{k}.{j}")
                    jb = E - 1
                    mka = p.add("dve", lambda e, jb=jb, k=k: e.tensor_add(
                        PSA[:, jb * 512:(jb + 1) * 512],
                        PSA[:, jb * 512:(jb + 1) * 512], A(MB3 + k * 512, 512)),
                        reads=[PR("PSA", jb), R(MB3 + k * 512, 512)],
                        writes=[PR("PSA", jb)], deps=[mm[jb], mb_ld[k]],
                        tag=f"mk{h}.{k}")
                    r_st = p.region("stats", 0, 8)
                    rmax = p.add("dve", lambda e, E=E: e.reduce_max(
                        stats[:, 0:1], PSA[:, 0:E * 512], axis=AX.X, negate=True),
                        reads=[PR("PSA", 0, E)], writes=[r_st],
                        deps=[mka] + list(mm.values()), tag=f"rm{h}.{k}")
                    rsc = p.add("dve", lambda e: e.tensor_scalar_mul(
                        stats[:, 0:1], stats[:, 0:1], float(128 ** -0.5)),
                        reads=[r_st], writes=[r_st], deps=[rmax])
                    exp_ = p.add("act", lambda e, E=E: e.activation(
                        out=A(PB3, E * 512), in_=PSA[:, 0:E * 512], func=AF.Exp,
                        bias=stats[:, 0:1], scale=float(128 ** -0.5),
                        accum_out=stats[:, 1:2]),
                        reads=[PR("PSA", 0, E), r_st], writes=[R(PB3, 2048), r_st],
                        deps=[rsc], tag=f"ex{h}.{k}")
                    rcp = p.add("dve", lambda e: e.reciprocal(stats[:, 2:3],
                                                              stats[:, 1:2]),
                                reads=[r_st], writes=[r_st], deps=[exp_])
                    pn = p.add("dve", lambda e, E=E: e.tensor_scalar(
                        out=A(PB3, E * 512), in0=A(PB3, E * 512),
                        scalar1=stats[:, 2:3], scalar2=None, op0=ALU.mult),
                        reads=[R(PB3, 2048), r_st], writes=[R(PB3, 2048)],
                        deps=[rcp], tag=f"pn{h}.{k}")
                    pt_t = {}
                    for sc in range(E * 4):
                        pt_t[sc] = emit_tr(PB3 + sc * 128, PTB3 + sc * 128,
                                           deps=[pn], slot=sc % 4,
                                           evac="act" if sc % 2 else "dve",
                                           tag=f"pb{h}.{k}.{sc}")
                    mmo = None
                    for sc in range(min(E * 4, NSB)):
                        mmo = p.add("pe", lambda e, sc=sc, E=E: e.matmul(
                            PSB[:, 512:640], A(VSB3 + sc * 128, 128),
                            A(PTB3 + sc * 128, 128), start=(sc == 0),
                            stop=(sc == min(E * 4, NSB) - 1)),
                            reads=[R(VSB3 + sc * 128, 128),
                                   R(PTB3 + sc * 128, 128)],
                            writes=[PR("PSB", 1)], deps=[vvld[sc], pt_t[sc]],
                            tag=f"ob{h}.{k}.{sc}")
                    ot_ev[(h, k)] = p.add("dve", lambda e, h=h, k=k: e.tensor_copy(
                        A(OUTT + h * TL + k * 128, 128), PSB[:, 512:640]),
                        reads=[PR("PSB", 1)],
                        writes=[R(OUTT + h * TL + k * 128, 128)], deps=[mmo],
                        tag=f"oe{h}.{k}")

            for db in range(D // 512):
                mm_last = {}
                for c in range(NHB):
                    sl = c % 2
                    wml = WQB + sl * 512
                    r_wm = R(wml, 512)
                    ldf = dma("qa", A32(WQF + sl * 1024, 512),
                              wo[c * 128:(c + 1) * 128, db * 512:(db + 1) * 512],
                              [], [R(WQF + sl * 1024, 1024)], tag=f"wo{db}.{c}")
                    cvw = p.add("dve", lambda e, wml=wml, sl=sl: e.tensor_copy(
                        A(wml, 512), A32(WQF + sl * 1024, 512)),
                        reads=[R(WQF + sl * 1024, 1024)], writes=[r_wm],
                        deps=[ldf], tag=f"wc{db}.{c}")
                    for q in range(NLT):
                        psn, ps_ap = ("PSA", PSA) if q < 4 else ("PSB", PSB)
                        bb = q % 4
                        ps = ps_ap[:, bb * 512:(bb + 1) * 512]
                        mm_last[q] = p.add(
                            "pe", lambda e, ps=ps, c=c, q=q, wml=wml: e.matmul(
                                ps, A(OUTT + c * TL + q * 128, 128), A(wml, 512),
                                start=(c == 0), stop=(c == NHB - 1)),
                            reads=[R(OUTT + c * TL + q * 128, 128), r_wm],
                            writes=[PR(psn, bb)], deps=[cvw, ot_ev[(c, q)]],
                            tag=f"wm{db}.{c}.{q}")
                for q in range(NLT):
                    psn, ps_ap = ("PSA", PSA) if q < 4 else ("PSB", PSB)
                    bb = q % 4
                    xs = A32(XS3 + (q % 2) * 1024, 512)
                    r_xs = R(XS3 + (q % 2) * 1024, 1024)
                    ld = dma("qs", xs[:, 0:512],
                             x_res[q * 128:(q + 1) * 128, db * 512:(db + 1) * 512],
                             [p.region("x_res", q * D + db * 512,
                                       q * D + (db + 1) * 512)],
                             [r_xs], tag=f"x3{db}.{q}")
                    ad = p.add("dve", lambda e, ps_ap=ps_ap, bb=bb, xs=xs: e.tensor_add(
                        xs[:, 0:512], xs[:, 0:512],
                        ps_ap[:, bb * 512:(bb + 1) * 512]),
                        reads=[r_xs, PR(psn, bb)], writes=[r_xs],
                        deps=[ld, mm_last[q]])
                    dma("qs", x_res[q * 128:(q + 1) * 128,
                                    db * 512:(db + 1) * 512],
                        xs[:, 0:512], [r_xs],
                        [p.region("x_res", q * D + db * 512,
                                  q * D + (db + 1) * 512)],
                        deps=[ad], tag=f"x3o{db}.{q}")

        if DEBUG_XRES:
            DBB = 91136
            for q in range(NLT):
                xsv = p.region("x_res", q * D, (q + 1) * D)
                tmp = R(DBB, 2048)
                for hf2 in range(2):
                    l1 = dma("qs", A32(DBB, 1024)[:, 0:D // 2],
                             x_res[q * 128:(q + 1) * 128,
                                   hf2 * (D // 2):(hf2 + 1) * (D // 2)],
                             [xsv], [tmp], tag=f"xd{q}.{hf2}a")
                    dma("qs", xdbg[q * 128:(q + 1) * 128,
                                   hf2 * (D // 2):(hf2 + 1) * (D // 2)],
                        A32(DBB, 1024)[:, 0:D // 2],
                        [tmp], [p.region("xdbg", q, q + 1)], deps=[l1],
                        tag=f"xd{q}.{hf2}b")

        if 4 in PHASES:
            emit_ffn(x_res, out, "out", 4)
        else:
            CPB = 91136
            for q in range(NLT):
                xsv = p.region("x_res", q * D, (q + 1) * D)
                tmp = R(CPB, 2048)
                l1 = dma("qs", A32(CPB, 1024)[:, 0:D // 2],
                         x_res[q * 128:(q + 1) * 128, 0:D // 2],
                         [xsv], [tmp], tag=f"cpo{q}a")
                dma("qs", out[q * 128:(q + 1) * 128, 0:D // 2],
                    A32(CPB, 1024)[:, 0:D // 2],
                    [tmp], [p.region("out", q * D, (q + 1) * D)], deps=[l1],
                    tag=f"cpo{q}b")
                l2 = dma("qs", A32(CPB, 1024)[:, 0:D // 2],
                         x_res[q * 128:(q + 1) * 128, D // 2:D],
                         [xsv], [tmp], tag=f"cpo{q}c")
                dma("qs", out[q * 128:(q + 1) * 128, D // 2:D],
                    A32(CPB, 1024)[:, 0:D // 2],
                    [tmp], [p.region("out", q * D, (q + 1) * D)], deps=[l2],
                    tag=f"cpo{q}d")

        section = p.emit_all(sems)

        @block.tensor
        def _(te):
            section(te, "tensor")

        @block.vector
        def _(ve):
            section(ve, "vector")

        @block.scalar
        def _(sc):
            section(sc, "scalar")

        @block.gpsimd
        def _(gp):
            section(gp, "gpsimd")

        @block.sync
        def _(sy):
            section(sy, "sync")

    return nc




def stage_inputs(cfg, inputs):
    x = np.ascontiguousarray(np.asarray(inputs["L0_attn_a_x"], dtype=np.float32))
    qgT = _int8_T(inputs["L0_ffn_qw_gate"])
    quT = _int8_T(inputs["L0_ffn_qw_up"])
    qdT = _int8_T(inputs["L0_ffn_qw_down"])
    zgT = np.ascontiguousarray(np.asarray(inputs["L0_ffn_z_gate"]).T.astype(np.int32))
    sgT = np.ascontiguousarray(np.asarray(inputs["L0_ffn_s_gate"]).T.astype(np.float32))
    zuT = np.ascontiguousarray(np.asarray(inputs["L0_ffn_z_up"]).T.astype(np.int32))
    suT = np.ascontiguousarray(np.asarray(inputs["L0_ffn_s_up"]).T.astype(np.float32))
    zdT = np.ascontiguousarray(np.asarray(inputs["L0_ffn_z_down"]).T.astype(np.int32))
    sdT = np.ascontiguousarray(np.asarray(inputs["L0_ffn_s_down"]).T.astype(np.float32))
    com = dict(
        qgT=qgT, quT=quT, qdT=qdT, zgT=zgT, sgT=sgT, zuT=zuT, suT=suT,
        zdT=zdT, sdT=sdT,
        bvq=np.asarray(inputs["L0_attn_a_bv_q"], dtype=np.float32),
        bvk=np.asarray(inputs["L0_attn_a_bv_k"], dtype=np.float32),
        bvv=np.asarray(inputs["L0_attn_a_bv_v"], dtype=np.float32),
        wq=np.asarray(inputs["L0_attn_b_wq"], dtype=np.float32),
        wk=np.asarray(inputs["L0_attn_b_wk"], dtype=np.float32),
        wv=np.asarray(inputs["L0_attn_b_wv"], dtype=np.float32),
        wo=np.asarray(inputs["L0_attn_b_wo"], dtype=np.float32),
        identity=np.eye(128, dtype=BF),
        runtag=np.full((2, 4), np.random.randint(1, 2**30), dtype=np.int32),
    )
    masks = [(_masks_a(cfg, par), _masks_b(cfg, par)) for par in (0, 1)]
    in_maps = []
    for c in range(2 * cfg["B"]):
        b, par = c // 2, c % 2
        rows = np.concatenate([np.arange(g * 128, (g + 1) * 128)
                               for g in cfg["sets"][par]])
        m = dict(com)
        m["x_my"] = np.ascontiguousarray(x[b][rows])
        m["x_full"] = np.ascontiguousarray(x[b])
        m["mska"], m["mskb"] = masks[par]
        in_maps.append(m)
    return in_maps


def gather_output(cfg, results):
    B, T, D = cfg["B"], cfg["T"], cfg["D"]
    out = np.empty((B, T, D), dtype=np.float32)
    for b in range(B):
        for g in range(cfg["NQT"]):
            par = cfg["owner"][g]
            li = cfg["lidx"][g]
            out[b, g * 128:(g + 1) * 128] = \
                results[2 * b + par]["out"][li * 128:(li + 1) * 128]
    return out


_NC_CACHE = {}


def kernel(**inputs):
    cfg = CFG
    key = "main"
    if key not in _NC_CACHE:
        _NC_CACHE[key] = build_program(cfg)
    nc = _NC_CACHE[key]
    in_maps = stage_inputs(cfg, inputs)
    res = run_bass_kernel_spmd(nc, in_maps, core_ids=list(range(2 * cfg["B"])))
    return gather_output(cfg, res.results)
